# version 17
# baseline (speedup 1.0000x reference)
import contextlib
import numpy as np
import concourse.bass as bass
import concourse.mybir as mybir
from concourse.bass_utils import run_bass_kernel_spmd

F32 = mybir.dt.float32
BF16 = mybir.dt.bfloat16
AF = mybir.ActivationFunctionType
ALU = mybir.AluOpType
AX = mybir.AxisListType

D = 1024
NH = 16
DH = 64
SEQ = 2048
NSEQ = 2
NCH = 4
NCHG = NSEQ * NCH
CH = 512
NT = 4
DCN = 8
HPN = 8
EPS = 1e-5

ENGINES = ("tensor", "vector", "scalar", "sync", "gpsimd")


class Sched:

    def __init__(self):
        self.ops = {e: [] for e in ENGINES}
        self.cnt = {}
        self.handles = {}

    def add_sem(self, name, handle):
        self.handles[name] = handle
        self.cnt[name] = 0

    CHAIN = {"vector": "s_dv", "scalar": "s_sc", "gpsimd": "s_gp", "tensor": "s_pe"}

    def op(self, eng, fn, wait=(), inc=None, k=1):
        n = None
        wait = list(wait)
        ch = self.CHAIN.get(eng)
        if ch is not None and k != 16:
            assert inc is None or inc == ch
            wait.append((ch, self.cnt[ch]))
            inc = ch
            k = 1
        if inc is not None:
            if k == 16:
                wait.append((inc, self.cnt[inc]))
            self.cnt[inc] += k
            n = self.cnt[inc]
        self.ops[eng].append((fn, tuple(wait), inc, k))
        return n

    def run(self, eng_name, eng):
        maxw = {}
        for fn, waits, inc, k in self.ops[eng_name]:
            for s, v in waits:
                if v > 0 and v > maxw.get(s, 0):
                    eng.wait_ge(self.handles[s], v)
                    maxw[s] = v
            instr = fn(eng)
            if inc is not None:
                if instr is None:
                    instr = eng.nop()
                instr.then_inc(self.handles[inc], k)


def build_program(phases=("ffn1", "attn", "ffn2")):
    do_ffn1 = "ffn1" in phases
    do_attn = "attn" in phases
    do_ffn2 = "ffn2" in phases

    nc = bass.Bass("TRN2", target_bir_lowering=False)

    x_in = nc.dram_tensor("x", [NSEQ * SEQ, D], F32, kind="ExternalInput").ap()
    out = nc.dram_tensor("out", [NSEQ * SEQ, D], F32, kind="ExternalOutput").ap()
    wts = nc.dram_tensor("wts", [D, D], F32, kind="ExternalInput").ap()
    wte = nc.dram_tensor("wte", [D, D], F32, kind="ExternalInput").ap()
    wtq = nc.dram_tensor("wtq", [D, D], F32, kind="ExternalInput").ap()
    wtk = nc.dram_tensor("wtk", [D, D], F32, kind="ExternalInput").ap()
    wtv = nc.dram_tensor("wtv", [D, D], F32, kind="ExternalInput").ap()
    wtp = nc.dram_tensor("wtp", [D, D], F32, kind="ExternalInput").ap()
    wg = nc.dram_tensor("wg", [D, 7], F32, kind="ExternalInput").ap()
    b_q = nc.dram_tensor("b_q", [1, D], F32, kind="ExternalInput").ap()
    b_k = nc.dram_tensor("b_k", [1, D], F32, kind="ExternalInput").ap()
    b_v = nc.dram_tensor("b_v", [1, D], F32, kind="ExternalInput").ap()
    b_p = nc.dram_tensor("b_p", [1, D], F32, kind="ExternalInput").ap()
    b_f = nc.dram_tensor("b_f", [1, D], F32, kind="ExternalInput").ap()
    wscr = nc.dram_tensor("wscr", [3, 2, 128, DCN, 512], BF16, kind="Internal").ap()

    S = Sched()

    with contextlib.ExitStack() as st:
        def sb(name, shape, dt):
            return st.enter_context(nc.sbuf_tensor(name, shape, dt))

        def mksem(name):
            S.add_sem(name, st.enter_context(nc.semaphore(name)))

        WS_T = sb("WS_T", [128, DCN, D], BF16)
        WE_T = sb("WE_T", [128, DCN, D], BF16)
        WP_T = sb("WP_T", [128, DCN, D], BF16)
        WSTR = sb("WSTR", [128, 2, DCN, 512], BF16)
        WG = sb("WG", [128, DCN, 7], BF16)
        BQR = sb("BQR", [1, D], BF16)
        BKR = sb("BKR", [1, D], BF16)
        BVR = sb("BVR", [1, D], BF16)
        BF2 = sb("BF2", [1, D], BF16)
        BPR = sb("BPR", [1, D], BF16)
        TRI = sb("TRI", [128, 128], BF16)
        IDE = sb("IDE", [128, 128], BF16)
        ONESR = sb("ONESR", [1, 512], BF16)
        ONESB = sb("ONESB", [65, 64], BF16)
        ONES16 = sb("ONES16", [16, 128], BF16)
        I16F = sb("I16F", [16, 16], F32)
        BD = sb("BD", [128, DCN, 16], BF16)

        for name in ("s_ld", "s_sw", "s_st", "s_of", "s_pe", "s_dv", "s_sc", "s_gp"):
            mksem(name)

        with nc.sbuf_tensor("STG", [128, DCN, D], F32) as STG, \
             nc.sbuf_tensor("STGB", [128, DCN, D], BF16) as STGB, \
             nc.sbuf_tensor("BSTG", [1, 6, D], F32) as BSTG, \
             nc.sbuf_tensor("WGS", [128, DCN, 7], F32) as WGS:

            def emit_prep():
                def gop(fn):
                    prev = S.cnt["s_gp"]
                    S.op("gpsimd", fn, wait=(("s_gp", prev),), inc="s_gp")
                gop(lambda g: g.memset(TRI[:, :], 0.0))
                gop(lambda g: g.affine_select(out=TRI[:, :], in_=TRI[:, :],
                                              compare_op=ALU.is_gt, fill=1.0, base=0,
                                              pattern=[[-1, 128]], channel_multiplier=1))
                gop(lambda g: g.memset(ONESR[:, :], 1.0))
                gop(lambda g: g.memset(ONESB[64:65, :], 1.0))
                gop(lambda g: g.memset(ONES16[:, :], 1.0))
                gop(lambda g: g.memset(IDE[:, :], 0.0))
                gop(lambda g: g.affine_select(out=IDE[:, :], in_=IDE[:, :],
                                              compare_op=ALU.not_equal, fill=1.0, base=0,
                                              pattern=[[-1, 128]], channel_multiplier=1))
                gop(lambda g: g.memset(I16F[:, :], 0.0))
                gop(lambda g: g.affine_select(out=I16F[:, :], in_=I16F[:, :],
                                              compare_op=ALU.not_equal, fill=1.0, base=0,
                                              pattern=[[-1, 16]], channel_multiplier=1))
                gop(lambda g: g.memset(BD[:, :, :], 0.0))

                def bdsets(g):
                    instr = None
                    for dc in range(DCN):
                        g.memset(BD[0:64, dc, 2 * dc:2 * dc + 1], 1.0)
                        instr = g.memset(BD[64:128, dc, 2 * dc + 1:2 * dc + 2], 1.0)
                    return instr
                gop(bdsets)

                S.op("sync", lambda e: e.dma_start(out=BSTG[0:1, 0, :], in_=b_q), inc="s_ld", k=16)
                S.op("sync", lambda e: e.dma_start(out=BSTG[0:1, 1, :], in_=b_k), inc="s_ld", k=16)
                S.op("sync", lambda e: e.dma_start(out=BSTG[0:1, 2, :], in_=b_v), inc="s_ld", k=16)
                S.op("sync", lambda e: e.dma_start(out=BSTG[0:1, 3, :], in_=b_f), inc="s_ld", k=16)
                S.op("sync", lambda e: e.dma_start(out=BSTG[0:1, 4, :], in_=b_p), inc="s_ld", k=16)
                bias_dma = S.op("sync", lambda e: e.dma_start(
                    out=WGS[:, :, :], in_=wg.rearrange("(c p) g -> p c g", p=128)), inc="s_ld", k=16)
                S.op("vector", lambda e: e.tensor_scalar_mul(BQR[0:1, :], BSTG[0:1, 0, :], 2.0 ** -1.5),
                     wait=(("s_ld", bias_dma),))
                S.op("vector", lambda e: e.tensor_scalar_mul(BKR[0:1, :], BSTG[0:1, 1, :], 2.0 ** -1.5))
                S.op("vector", lambda e: e.tensor_copy(BVR[0:1, :], BSTG[0:1, 2, :]))
                S.op("vector", lambda e: e.tensor_scalar_mul(BF2[:, :], BSTG[0:1, 3, :], 0.5))
                S.op("vector", lambda e: e.tensor_copy(BPR[:, :], BSTG[0:1, 4, :]))
                S.op("vector", lambda e: e.tensor_copy(WG[:, :, :], WGS[:, :, :]), inc="s_dv")

                RS8 = 2.0 ** -1.5
                plan = [(wts, WS_T, 0.5, None), (wte, WE_T, 0.5, None),
                        (wtp, WP_T, 1.0, None), (wtq, None, RS8, 0),
                        (wtk, None, RS8, 1), (wtv, None, 1.0, 2)]
                prev_cast = 0
                prev_store = 0
                for i, (src, dst, scale, scr_i) in enumerate(plan):
                    ld = S.op("sync", lambda e, s=src: e.dma_start(
                        out=STG[:, :, :], in_=s.rearrange("(c p) o -> p c o", p=128)),
                        wait=(("s_dv", prev_cast),), inc="s_ld", k=16)
                    if dst is not None:
                        prev_cast = S.op("vector", lambda e, d=dst, sc=scale: e.tensor_scalar_mul(
                            d[:, :, :], STG[:, :, :], sc), wait=(("s_ld", ld),), inc="s_dv")
                    else:
                        prev_cast = S.op("vector", lambda e, sc=scale: e.tensor_scalar_mul(
                            STGB[:, :, :], STG[:, :, :], sc),
                            wait=(("s_ld", ld), ("s_ld", prev_store)), inc="s_dv")
                        for half in range(2):
                            prev_store = S.op("sync", lambda e, j=scr_i, h=half: e.dma_start(
                                out=wscr[j, h], in_=STGB[:, :, h * 512:(h + 1) * 512]),
                                wait=(("s_dv", prev_cast),), inc="s_ld", k=16)
                return S.cnt["s_dv"], S.cnt["s_ld"]

            prep_dv, prep_ld = emit_prep()

        KT = sb("KT", [128, HPN, SEQ], BF16)
        VB = sb("VB", [128, 16, NH, 65], BF16)
        QT = sb("QT", [128, HPN, CH], BF16)
        XC = sb("XC", [128, NT, D], F32)
        XN = sb("XN", [128, 3, D], BF16)
        XNT = sb("XNT", [128, DCN, CH], BF16)
        OT = sb("OT", [128, HPN, CH], BF16)
        OTMP = sb("OTMP", [64, 512], BF16)
        PT = sb("PT", [128, 8, 512], BF16)
        JUNK = sb("JUNK", [128, 512], F32)
        SQS = sb("SQS", [128, 2, 512], BF16)
        RBB = sb("RBB", [64, 512], F32)
        SGR = sb("SGR", [65, 512], F32)
        RGB = sb("RGB", [65, 2, 512], BF16)
        BNS = sb("BNS", [128, NT, 12], F32)
        STAT = sb("STAT", [128, 6, NT], F32)
        SMW = sb("SMW", [128, NT, 4], F32)
        GD = sb("GD", [128, NT, 8], F32)
        GEX = sb("GEX", [128, NT, 8], F32)
        GW = sb("GW", [128, 13], F32)
        GC = sb("GC", [128, 8], F32)
        GGATE = sb("GGATE", [128, NT], F32)
        NRM = sb("NRM", [16, 8], F32)
        D16 = sb("D16", [16, 16], BF16)
        MBC = sb("MBC", [128, 16], F32)
        MBCC = sb("MBCC", [128, 1], F32)

        PSA = st.enter_context(nc.psum_tensor("PSA", [128, 4096], F32))

        def bank(b, w=512, off=0):
            return PSA[:, b * 512 + off: b * 512 + off + w]

        S.op("gpsimd", lambda e: e.memset(VB[:, :, :, 64:65], 1.0),
             wait=(("s_dv", prep_dv), ("s_ld", prep_ld), ("s_gp", S.cnt["s_gp"])), inc="s_gp")
        gp_done = S.cnt["s_gp"]

        load_ms = {}
        store_ms = {}
        trans_ms = {}
        xnt_free = [0]
        xn_free = {}
        xn_rot = [0]
        tslot_free = {0: 0, 1: 0}

        for t in range(NT):
            load_ms[(0, t)] = S.op("sync", lambda e, t=t: e.dma_start(
                out=XC[:, t, :], in_=x_in[t * 128:(t + 1) * 128, :]),
                wait=(("s_dv", prep_dv), ("s_gp", gp_done)) if t == 0 else (),
                inc="s_ld", k=16)

        def make_ln(g, ln):
            def stats(t):
                w = []
                if ln == 0:
                    w.append(("s_ld", load_ms[(g, t)]))
                S.op("vector", lambda e, t=t: e.bn_stats(BNS[:, t, 0:6], XC[:, t, 0:512]), wait=w)
                S.op("vector", lambda e, t=t: e.bn_stats(BNS[:, t, 6:12], XC[:, t, 512:1024]))
                mvt = BNS[:, t, 1:12:3]
                vvt = BNS[:, t, 2:12:3]
                S.op("vector", lambda e, t=t, mvt=mvt: e.tensor_mul(SMW[:, t, :], mvt, mvt))
                S.op("vector", lambda e, t=t, vvt=vvt: e.scalar_tensor_tensor(
                    SMW[:, t, :], SMW[:, t, :], 256.0, vvt, ALU.mult, ALU.add))
                S.op("vector", lambda e, t=t: e.tensor_reduce(
                    STAT[:, 5, t:t + 1], SMW[:, t, :], axis=AX.X, op=ALU.add))
                S.op("vector", lambda e, t=t, mvt=mvt: e.tensor_reduce(
                    STAT[:, 0, t:t + 1], mvt, axis=AX.X, op=ALU.add))
                S.op("vector", lambda e, t=t: e.tensor_scalar_mul(
                    STAT[:, 0, t:t + 1], STAT[:, 0, t:t + 1], 0.25))
                S.op("vector", lambda e, t=t: e.tensor_mul(
                    STAT[:, 1, t:t + 1], STAT[:, 0, t:t + 1], STAT[:, 0, t:t + 1]))
                S.op("vector", lambda e, t=t: e.scalar_tensor_tensor(
                    STAT[:, 2, t:t + 1], STAT[:, 5, t:t + 1], 1.0 / 1024.0,
                    STAT[:, 1, t:t + 1], ALU.mult, ALU.subtract))
                return S.op("vector", lambda e, t=t: e.tensor_scalar(
                    STAT[:, 2, t:t + 1], STAT[:, 2, t:t + 1], EPS, None, ALU.add))

            def lnexp(t, dvv):
                S.op("scalar", lambda e, t=t: e.activation(
                    STAT[:, 3, t:t + 1], STAT[:, 2, t:t + 1], AF.Ln),
                    wait=(("s_dv", dvv),))
                return S.op("scalar", lambda e, t=t: e.activation(
                    STAT[:, 4, t:t + 1], STAT[:, 3, t:t + 1], AF.Exp, scale=-0.5))

            slot_of_t = {}

            def apply_t(t, scm):
                slot = xn_rot[0] % 3
                xn_rot[0] += 1
                slot_of_t[t] = slot
                w = [("s_sc", scm)] + list(xn_free.get(slot, ()))
                ap_ms = S.op("vector", lambda e, t=t, slot=slot: e.tensor_scalar(
                    XN[:, slot, :], XC[:, t, :], STAT[:, 0, t:t + 1], STAT[:, 4, t:t + 1],
                    ALU.subtract, ALU.mult), wait=w)
                return ap_ms

            def transpose_t(t, ap_ms):
                slot = slot_of_t[t]
                tb = (6 + t % 2) * 512
                last_pe = 0
                for dc in range(DCN):
                    pv = PSA[:, tb + dc * 64:tb + dc * 64 + 64].bitcast(BF16)
                    last_pe = S.op("tensor", lambda e, slot=slot, dc=dc, pv=pv: e.transpose(
                        pv, XN[:, slot, dc * 128:(dc + 1) * 128], IDE[:, :]),
                        wait=(("s_dv", ap_ms), ("s_dv", tslot_free[t % 2])), inc="s_pe")
                cp = S.op("vector", lambda e, t=t, tb=tb: e.tensor_copy(
                    XNT[:, :, t * 128:(t + 1) * 128],
                    PSA[:, tb:tb + 512].bitcast(BF16).rearrange(
                        "p (c f) -> p c f", c=DCN)),
                    wait=(("s_pe", last_pe), ("s_pe", xnt_free[0])))
                tslot_free[t % 2] = cp
                trans_ms[(g, ln, t)] = (("s_dv", cp),)
                xn_free[slot] = (("s_pe", last_pe),)
                return cp

            exps = {}
            applied = {}

            def pre(t):
                if t not in exps:
                    exps[t] = lnexp(t, stats(t))

            def full(t):
                if t in applied:
                    return
                pre(t)
                applied[t] = True
                transpose_t(t, apply_t(t, exps[t]))

            return {"pre": pre, "full": full, "exps": exps, "applied": applied,
                    "apply": apply_t, "transpose": transpose_t}

        def emit_ffn(g, ln, entry_dv, lnh, tail=None):
            evac_ms = {}
            gate_ms = {}
            mm_ms = {}
            lnh["full"](0)
            for t in range(NT):
                lnh["pre"](t)
            ap = {}
            if 1 not in lnh["applied"]:
                lnh["applied"][1] = True
                ap[1] = lnh["apply"](1, lnh["exps"][1])
            for u, (t, oh) in enumerate([(t, oh) for t in range(NT) for oh in range(2)]):
                slot = u % 2
                bs, be = slot * 2, slot * 2 + 1
                gb = 4 + (t % 2)
                common = list(trans_ms[(g, ln, t)])
                if u >= 2:
                    common.append(("s_dv", evac_ms[u - 2]))
                else:
                    common.append(("s_dv", entry_dv))
                if oh == 0 and t >= 2:
                    common.append(gate_ms[t - 2])

                def mms(e, t=t, oh=oh, bs=bs, be=be, gb=gb):
                    tl = slice(t * 128, (t + 1) * 128)
                    ohs = slice(oh * 512, (oh + 1) * 512)
                    for dc in range(DCN):
                        e.matmul(bank(bs), XNT[:, dc, tl], WS_T[:, dc, ohs],
                                 start=(dc == 0), stop=False)
                    e.matmul(bank(bs), ONESR[0:1, 0:128], BF2[0:1, ohs],
                             start=False, stop=True)
                    for dc in range(DCN):
                        instr = e.matmul(bank(be), XNT[:, dc, tl], WE_T[:, dc, ohs],
                                         start=(dc == 0), stop=(dc == DCN - 1))
                    if oh == 0:
                        for dc in range(DCN):
                            instr = e.matmul(bank(gb, 7, (t // 2) * 8), XNT[:, dc, tl],
                                             WG[:, dc, :], start=(dc == 0),
                                             stop=(dc == DCN - 1))
                    return instr
                mm_ms[u] = S.op("tensor", mms, wait=common, inc="s_pe")

                if oh == 0 and t + 1 < NT:
                    if (t + 1) in ap:
                        lnh["transpose"](t + 1, ap[t + 1])
                    else:
                        lnh["full"](t + 1)
                    if t + 2 < NT and (t + 2) not in lnh["applied"]:
                        lnh["applied"][t + 2] = True
                        ap[t + 2] = lnh["apply"](t + 2, lnh["exps"][t + 2])

                if oh == 0:
                    gv = bank(gb, 7, (t // 2) * 8)
                    S.op("vector", lambda e, gv=gv: e.tensor_reduce(
                        GC[:, 0:1], gv, axis=AX.X, op=ALU.max), wait=(("s_pe", mm_ms[u]),))
                    d1 = S.op("vector", lambda e, gv=gv, t=t: e.tensor_scalar(
                        GD[:, t, 0:7], gv, GC[:, 0:1], None, ALU.subtract))
                    s1 = S.op("scalar", lambda e, t=t: e.activation(
                        GEX[:, t, 0:7], GD[:, t, 0:7], AF.Exp), wait=(("s_dv", d1),), inc="s_sc")
                    S.op("vector", lambda e, t=t: e.tensor_reduce(
                        GC[:, 1:2], GEX[:, t, 0:7], axis=AX.X, op=ALU.add), wait=(("s_sc", s1),))
                    S.op("vector", lambda e, t=t: e.tensor_single_scalar(
                        GW[:, 0:7], GD[:, t, 0:7], 0.0, ALU.is_equal))
                    S.op("vector", lambda e, t=t: e.tensor_sub(GW[:, 0:7], GEX[:, t, 0:7], GW[:, 0:7]))
                    S.op("vector", lambda e: e.tensor_reduce(
                        GC[:, 2:3], GW[:, 0:7], axis=AX.X, op=ALU.max))
                    S.op("vector", lambda e: e.tensor_scalar(
                        GC[:, 3:4], GC[:, 2:3], 1.0, None, ALU.add))
                    S.op("vector", lambda e: e.reciprocal(GC[:, 5:6], GC[:, 1:2]))
                    S.op("vector", lambda e: e.tensor_mul(GC[:, 4:5], GC[:, 3:4], GC[:, 5:6]))
                    S.op("vector", lambda e: e.reciprocal(GC[:, 6:7], GC[:, 4:5]))
                    gate_ms[t] = ("s_dv", S.op("vector", lambda e, t=t: e.tensor_mul(
                        GGATE[:, t:t + 1], GC[:, 4:5], GC[:, 6:7])))

                ohs = slice(oh * 512, (oh + 1) * 512)
                S.op("vector", lambda e, t=t, be=be, ohs=ohs: e.scalar_tensor_tensor(
                    JUNK[:, :], bank(be), GGATE[:, t:t + 1], XC[:, t, ohs],
                    ALU.mult, ALU.add), wait=(("s_pe", mm_ms[u]), gate_ms[t]))
                evac_ms[u] = S.op("vector", lambda e, t=t, bs=bs, ohs=ohs: e.scalar_tensor_tensor(
                    XC[:, t, ohs], bank(bs), 1.0, JUNK[:, :], ALU.mult, ALU.add))
                if oh == 1 and tail is not None:
                    tail(t, evac_ms[u])
            xnt_free[0] = S.cnt["s_pe"]
            return {t: evac_ms[2 * t + 1] for t in range(NT)}

        def emit_qkv(g, slot_free, entry_dv):
            c = g % NCH
            ld = {}

            def stream(i, widx, half, wait_pe):
                ld[i] = S.op("scalar", lambda e, widx=widx, half=half, i=i: e.dma_start(
                    out=WSTR[:, i % 2, :, :], in_=wscr[widx, half]),
                    wait=(("s_pe", wait_pe), ("s_ld", prep_ld)), inc="s_sw", k=16)

            stream(0, 0, 0, slot_free[0])
            stream(1, 0, 1, slot_free[1])

            qmm, qev = {}, {}
            for hp in range(HPN):
                b = hp % 2
                w = [("s_sw", ld[0 if hp < 4 else 1])] + list(trans_ms[(g, 1, "all")])
                if hp >= 2:
                    w.append(qev[hp - 2])
                else:
                    w.append(("s_dv", entry_dv))

                def qm(e, hp=hp, b=b):
                    sl = 0 if hp < 4 else 1
                    col = slice((hp % 4) * 128, (hp % 4 + 1) * 128)
                    for dc in range(DCN):
                        e.matmul(bank(b), WSTR[:, sl, dc, col], XNT[:, dc, :],
                                 start=(dc == 0), stop=False)
                    return e.matmul(bank(b), BQR[0:1, hp * 128:(hp + 1) * 128],
                                    ONESR[0:1, 0:512], start=False, stop=True)
                qmm[hp] = S.op("tensor", qm, wait=w, inc="s_pe")
                qev[hp] = ("s_dv", S.op("vector", lambda e, hp=hp, b=b: e.tensor_copy(
                    QT[:, hp, :], bank(b)), wait=(("s_pe", qmm[hp]),)))

            stream(2, 1, 0, qmm[3])
            stream(3, 1, 1, qmm[7])

            kmm, kev = {}, {}
            for hp in range(HPN):
                b = 2 + hp % 2
                w = [("s_sw", ld[2 if hp < 4 else 3])]
                if hp >= 2:
                    w.append(kev[hp - 2])

                def km(e, hp=hp, b=b):
                    sl = 0 if hp < 4 else 1
                    col = slice((hp % 4) * 128, (hp % 4 + 1) * 128)
                    for dc in range(DCN):
                        e.matmul(bank(b), WSTR[:, sl, dc, col], XNT[:, dc, :],
                                 start=(dc == 0), stop=False)
                    return e.matmul(bank(b), BKR[0:1, hp * 128:(hp + 1) * 128],
                                    ONESR[0:1, 0:512], start=False, stop=True)
                kmm[hp] = S.op("tensor", km, wait=w, inc="s_pe")
                kev[hp] = ("s_dv", S.op("vector", lambda e, hp=hp, b=b, c=c: e.tensor_copy(
                    KT[:, hp, c * 512:(c + 1) * 512], bank(b)),
                    wait=(("s_pe", kmm[hp]),)))

            stream(4, 2, 0, kmm[3])
            stream(5, 2, 1, kmm[7])

            vmm, vev = {}, {}
            for t in range(NT):
                pb = 4 + 2 * (t % 2)
                w = [("s_sw", ld[5])]
                if t >= 2:
                    w.append(vev[t - 2])

                def vm(e, t=t, pb=pb):
                    tl = slice(t * 128, (t + 1) * 128)
                    instr = None
                    for half in range(2):
                        for dc in range(DCN):
                            e.matmul(bank(pb + half), XNT[:, dc, tl], WSTR[:, half, dc, :],
                                     start=(dc == 0), stop=False)
                        instr = e.matmul(bank(pb + half), ONESR[0:1, 0:128],
                                         BVR[0:1, half * 512:(half + 1) * 512],
                                         start=False, stop=True)
                    return instr
                vmm[t] = S.op("tensor", vm, wait=w, inc="s_pe")
                jt = c * 4 + t
                vev[t] = ("s_dv", S.op("vector", lambda e, jt=jt, pb=pb: e.tensor_copy(
                    VB[:, jt, :, 0:64],
                    PSA[:, pb * 512:(pb + 2) * 512].rearrange("p (h v) -> p h v", h=16)),
                    wait=(("s_pe", vmm[t]),)))

            slot_free[0] = vmm[NT - 1]
            slot_free[1] = vmm[NT - 1]
            xnt_free[0] = S.cnt["s_pe"]
            return qev, kev, vev

        def emit_norms(g, qev, kev, vev):
            c = g % NCH
            nmm = {}
            sq_ms = {}
            for i, which in enumerate(("q", "k")):
                for hp in range(HPN):
                    gidx = i * 8 + hp
                    w = [qev[hp] if which == "q" else kev[hp]]
                    if gidx >= 2:
                        pi, php = divmod(gidx - 2, 8)
                        w.append(("s_pe", nmm[(pi, php)]))
                    src = (lambda hp=hp: QT[:, hp, :]) if which == "q" else \
                          (lambda hp=hp, c=c: KT[:, hp, c * 512:(c + 1) * 512])
                    sq_ms[(i, hp)] = S.op("vector", lambda e, hp=hp, src=src: e.tensor_mul(
                        SQS[:, hp % 2, :], src(), src()), wait=w)
                    wpe = [("s_dv", sq_ms[(i, hp)])]
                    if i == 0 and hp < 2:
                        wpe.append(vev[NT - 1])
                    nmm[(i, hp)] = S.op("tensor", lambda e, i=i, hp=hp: e.matmul(
                        PSA[0:16, i * 512:(i + 1) * 512], BD[:, hp, :], SQS[:, hp % 2, :],
                        start=(hp == 0), stop=(hp == HPN - 1)), wait=wpe, inc="s_pe")
            S.op("vector", lambda e: e.reduce_max(NRM[:, 0:1], PSA[0:16, 0:512], axis=AX.X),
                 wait=(("s_pe", nmm[(1, HPN - 1)]),))
            S.op("vector", lambda e: e.reduce_max(NRM[:, 1:2], PSA[0:16, 512:1024], axis=AX.X))
            if c == 0:
                S.op("vector", lambda e: e.tensor_copy(NRM[:, 2:3], NRM[:, 1:2]))
            else:
                S.op("vector", lambda e, c=c: e.tensor_tensor(
                    NRM[:, 2 + c % 2:3 + c % 2], NRM[:, 2 + (c + 1) % 2:3 + (c + 1) % 2],
                    NRM[:, 1:2], ALU.max))
            S.op("vector", lambda e, c=c: e.tensor_add(NRM[:, 4:5], NRM[:, 0:1],
                                                       NRM[:, 2 + c % 2:3 + c % 2]))
            dvm = S.op("vector", lambda e: e.tensor_scalar(
                D16[:, :], I16F[:, :], NRM[:, 4:5], -0.55, ALU.mult, ALU.mult), inc="s_dv")
            mh = S.op("tensor", lambda e: e.matmul(
                PSA[:, 1024:1040], ONES16[:, :], D16[:, :], start=True, stop=True),
                wait=(("s_dv", dvm),), inc="s_pe")
            cp = S.op("scalar", lambda e: e.copy(MBC[:, :], PSA[:, 1024:1040]),
                      wait=(("s_pe", mh),), inc="s_sc")
            mbc = S.op("vector", lambda e: e.tensor_reduce(
                MBCC[:, :], MBC[:, :], axis=AX.X, op=ALU.min),
                wait=(("s_sc", cp),))
            return mbc

        def emit_attention(g, mbc_ms, attn_state):
            c = g % NCH
            kend = 4 * (c + 1)
            JP = kend // 2
            sb_free = attn_state["sb_free"]
            pt_last_pv = attn_state["pt_last_pv"]
            o_free_dv = attn_state["o_free_dv"]
            r_free_sc = attn_state["r_free_sc"]
            of_last = attn_state["of_last"]
            pend = attn_state["pend"]

            def off_of(j):
                return (j - 4 * c) * 128 if j >= 4 * c else 0

            def ob(h):
                return 4 + h % 2

            def rb(h):
                return 6 + h % 2

            def emit_ss(he, ho, jp):
                hp = he // 2
                w = [("s_sc", sb_free[0]), ("s_dv", mbc_ms)]

                def fn(e, hp=hp, jp=jp):
                    instr = None
                    for h in (he, ho):
                        hb = 64 * (h % 2)
                        for j in (2 * jp, 2 * jp + 1):
                            off = off_of(j)
                            b = 2 * (h % 2) + (j % 2)
                            instr = e.matmul(bank(b, 512 - off, off),
                                             KT[hb:hb + 64, hp, j * 128:(j + 1) * 128],
                                             QT[hb:hb + 64, hp, off:512],
                                             start=True, stop=True)
                    return instr
                return S.op("tensor", fn, wait=w, inc="s_pe")

            def emit_expb(jp, s_ms):
                w = [("s_pe", s_ms), ("s_pe", pt_last_pv.get(jp % 2, 0))]
                o = PT.rearrange("p (a b) f -> p a b f", a=2)[:, :, 2 * (jp % 2):2 * (jp % 2) + 2, :]
                i = PSA[:, 0:2048].rearrange("p (a b c) -> p a b c", a=2, b=2)
                return S.op("scalar", lambda e, o=o, i=i: e.activation(
                    o, i, AF.Exp, bias=MBCC[:, :]), wait=w, inc="s_sc")

            def emit_masks(h, jp, e_ms):
                ms = 0
                for j in (2 * jp, 2 * jp + 1):
                    if j >= 4 * c:
                        off = off_of(j)
                        sl = 4 * (h % 2) + (j % 4)
                        ms = S.op("vector", lambda e, sl=sl, off=off: e.tensor_mul(
                            PT[:, sl, off:off + 128], PT[:, sl, off:off + 128], TRI[:, :]),
                            wait=(("s_sc", e_ms),))
                return ms

            def emit_pvv(he, ho, jp, ee, eo, me, mo):
                w = [("s_sc", ee), ("s_sc", eo)]
                if me:
                    w.append(("s_dv", me))
                if mo:
                    w.append(("s_dv", mo))
                if jp == 0:
                    w.append(("s_dv", o_free_dv[0]))
                    w.append(("s_dv", o_free_dv[1]))

                def fn(e, he=he, ho=ho, jp=jp):
                    instr = None
                    for h in (he, ho):
                        for j in (2 * jp, 2 * jp + 1):
                            off = off_of(j)
                            sl = 4 * (h % 2) + (j % 4)
                            instr = e.matmul(
                                PSA[0:65, ob(h) * 512 + off:(ob(h) + 1) * 512],
                                VB[:, j, h, :], PT[:, sl, off:512],
                                start=(j == 0), stop=(j == kend - 1))
                    return instr
                pv = S.op("tensor", fn, wait=w, inc="s_pe")
                pt_last_pv[jp % 2] = pv
                return pv

            def emit_sg_chain(h, pv_ms):
                S.op("scalar", lambda e, h=h: e.activation(
                    SGR[64:65, 0:512], PSA[64:65, ob(h) * 512:(ob(h) + 1) * 512], AF.Ln),
                    wait=(("s_pe", pv_ms),))
                rgb = S.op("scalar", lambda e, h=h: e.activation(
                    RGB[64:65, h % 2, :], SGR[64:65, 0:512], AF.Exp, scale=-1.0))
                return {"h": h, "rgb_sc": rgb}

            def emit_finalize(p, rbb_free_dv):
                h = p["h"]
                r1 = S.op("tensor", lambda e, h=h: e.matmul(
                    PSA[0:64, rb(h) * 512:(rb(h) + 1) * 512], ONESB[64:65, :],
                    RGB[64:65, h % 2, :], start=True, stop=True),
                    wait=(("s_sc", p["rgb_sc"]), r_free_sc[h % 2]), inc="s_pe")
                rbc = S.op("vector", lambda e, h=h: e.tensor_copy(
                    RBB[:, :], PSA[0:64, rb(h) * 512:(rb(h) + 1) * 512]),
                    wait=(("s_pe", r1), ("s_dv", rbb_free_dv)))
                r_free_sc[h % 2] = ("s_dv", rbc)
                hp, hb = h // 2, 64 * (h % 2)
                if hb == 0:
                    ot = S.op("vector", lambda e, hp=hp, h=h: e.tensor_tensor(
                        OT[0:64, hp, :], PSA[0:64, ob(h) * 512:(ob(h) + 1) * 512],
                        RBB[:, :], ALU.mult))
                else:
                    ot = S.op("vector", lambda e, h=h: e.tensor_tensor(
                        OTMP[:, :], PSA[0:64, ob(h) * 512:(ob(h) + 1) * 512],
                        RBB[:, :], ALU.mult),
                        wait=(("s_of", of_last[0]),))
                    of_last[0] = S.op("sync", lambda e, hp=hp: e.dma_start(
                        out=OT[64:128, hp, :], in_=OTMP[:, :]),
                        wait=(("s_dv", ot),), inc="s_of", k=16)
                o_free_dv[h % 2] = ot
                return ot

            def emit_pvss(he, ho, jp, ee, me, mo):
                w = [("s_sc", ee)]
                if me:
                    w.append(("s_dv", me))
                if mo:
                    w.append(("s_dv", mo))
                if jp == 1:
                    w.append(("s_dv", o_free_dv[0]))
                    w.append(("s_dv", o_free_dv[1]))
                hp = he // 2

                def fn(e, he=he, ho=ho, jp=jp, hp=hp):
                    for h in (he, ho):
                        for j in (2 * jp - 2, 2 * jp - 1):
                            off = off_of(j)
                            sl = 4 * (h % 2) + (j % 4)
                            e.matmul(PSA[0:65, ob(h) * 512 + off:(ob(h) + 1) * 512],
                                     VB[:, j, h, :], PT[:, sl, off:512],
                                     start=(j == 0), stop=(j == kend - 1))
                    instr = None
                    for h in (he, ho):
                        hb = 64 * (h % 2)
                        for j in (2 * jp, 2 * jp + 1):
                            off = off_of(j)
                            b = 2 * (h % 2) + (j % 2)
                            instr = e.matmul(bank(b, 512 - off, off),
                                             KT[hb:hb + 64, hp, j * 128:(j + 1) * 128],
                                             QT[hb:hb + 64, hp, off:512],
                                             start=True, stop=True)
                    return instr
                pv = S.op("tensor", fn, wait=w, inc="s_pe")
                pt_last_pv[(jp - 1) % 2] = pv
                return pv

            ot_last = 0
            for hp2 in range(HPN):
                he, ho = 2 * hp2, 2 * hp2 + 1
                s_ms = emit_ss(he, ho, 0)
                if pend[0] is not None:
                    pe_, po_ = pend[0]
                    o1 = emit_finalize(pe_, 0)
                    ot_last = emit_finalize(po_, o1)
                    pend[0] = None
                ee = emit_expb(0, s_ms)
                me = emit_masks(he, 0, ee)
                mo = emit_masks(ho, 0, ee)
                sb_free[0] = ee
                prev = (0, ee, ee, me, mo)
                for jp in range(1, JP):
                    pjp, pee, peo, pme, pmo = prev
                    s_ms = emit_pvss(he, ho, jp, pee, pme, pmo)
                    ee = emit_expb(jp, s_ms)
                    me = emit_masks(he, jp, ee)
                    mo = emit_masks(ho, jp, ee)
                    sb_free[0] = ee
                    prev = (jp, ee, ee, me, mo)
                pjp, pee, peo, pme, pmo = prev
                pv = emit_pvv(he, ho, pjp, pee, peo, pme, pmo)
                pend[0] = (emit_sg_chain(he, pv), emit_sg_chain(ho, pv))
            pe_, po_ = pend[0]
            o1 = emit_finalize(pe_, 0)
            ot_last = emit_finalize(po_, o1)
            pend[0] = None
            return ot_last

        def emit_delta(g, ot_dv, tail=None):
            ev = {}
            sc_now = S.cnt["s_sc"]
            of_now = S.cnt["s_of"]
            for t in range(NT):
                pb = 2 * (t % 2)
                w = [("s_dv", ot_dv), ("s_of", of_now)]
                if t >= 2:
                    w.append(("s_dv", ev[t - 2]))
                else:
                    w.append(("s_sc", sc_now))

                def dm(e, t=t, pb=pb):
                    tl = slice(t * 128, (t + 1) * 128)
                    instr = None
                    for half in range(2):
                        for hp in range(DCN):
                            e.matmul(bank(pb + half), OT[:, hp, tl],
                                     WP_T[:, hp, half * 512:(half + 1) * 512],
                                     start=(hp == 0), stop=False)
                        instr = e.matmul(bank(pb + half), ONESR[0:1, 0:128],
                                         BPR[0:1, half * 512:(half + 1) * 512],
                                         start=False, stop=True)
                    return instr
                m = S.op("tensor", dm, wait=w, inc="s_pe")

                S.op("vector", lambda e, t=t, pb=pb: e.tensor_add(
                    XC[:, t, 0:512], XC[:, t, 0:512], bank(pb)), wait=(("s_pe", m),))
                ev[t] = S.op("vector", lambda e, t=t, pb=pb: e.tensor_add(
                    XC[:, t, 512:1024], XC[:, t, 512:1024], bank(pb + 1)))
                if tail is not None:
                    tail(t, ev[t])
            return ev

        def store_load_t(g, t, done_ms):
            base = g * CH + t * 128
            store_ms[(g, t)] = S.op("sync", lambda e, base=base, t=t: e.dma_start(
                out=out[base:base + 128, :], in_=XC[:, t, :]),
                wait=(("s_dv", done_ms),), inc="s_st", k=16)
            if g + 1 < NCHG:
                nbase = (g + 1) * CH + t * 128
                load_ms[(g + 1, t)] = S.op("sync", lambda e, nbase=nbase, t=t: e.dma_start(
                    out=XC[:, t, :], in_=x_in[nbase:nbase + 128, :]),
                    wait=(("s_st", store_ms[(g, t)]),), inc="s_ld", k=16)

        slot_free = [0, 0]
        attn_state = {"sb_free": {0: 0, 1: 0},
                      "pt_last_pv": {},
                      "o_free_dv": {0: 0, 1: 0},
                      "r_free_sc": {0: ("s_dv", 0), 1: ("s_dv", 0)},
                      "of_last": [0], "pend": [None]}
        LNH = {}

        def get_lnh(g, ln):
            if (g, ln) not in LNH:
                LNH[(g, ln)] = make_ln(g, ln)
            return LNH[(g, ln)]

        ln_next = {0: (1 if do_attn else (2 if do_ffn2 else None)),
                   1: (2 if do_ffn2 else None), 2: None}

        def mk_tail(g, ln):
            nxt = ln_next[ln]
            if nxt is not None:
                return lambda t, dv_ms: get_lnh(g, nxt)["pre"](t)

            def last_tail(t, dv_ms):
                store_load_t(g, t, dv_ms)
                if g + 1 < NCHG:
                    get_lnh(g + 1, 0)["pre"](t)
            return last_tail

        for g in range(NCHG):
            done = None
            guard = S.cnt["s_dv"]
            if do_ffn1:
                done = emit_ffn(g, 0, guard, get_lnh(g, 0), mk_tail(g, 0))
            elif do_attn or do_ffn2:
                pass
            if do_attn:
                guard = S.cnt["s_dv"]
                lnh2 = get_lnh(g, 1)
                for t in range(NT):
                    lnh2["full"](t)
                trans_ms[(g, 1, "all")] = tuple(
                    trans_ms[(g, 1, t)][0] for t in range(NT))
                qev, kev, vev = emit_qkv(g, slot_free, guard)
                mbc = emit_norms(g, qev, kev, vev)
                ot_dv = emit_attention(g, mbc, attn_state)
                done = emit_delta(g, ot_dv, mk_tail(g, 1))
            if do_ffn2:
                guard = S.cnt["s_dv"]
                done = emit_ffn(g, 2, guard, get_lnh(g, 2), mk_tail(g, 2))
            if done is None:
                done = {t: S.cnt["s_dv"] for t in range(NT)}
            for t in range(NT):
                if (g, t) not in store_ms:
                    store_load_t(g, t, done[t] if isinstance(done, dict) else S.cnt["s_dv"])
                    if g + 1 < NCHG:
                        get_lnh(g + 1, 0)["pre"](t)

        fw = [("s_st", S.cnt["s_st"]), ("s_ld", S.cnt["s_ld"]),
              ("s_sw", S.cnt["s_sw"]), ("s_of", S.cnt["s_of"])]
        S.op("sync", lambda e: e.nop(), wait=tuple(fw))

        with nc.Block() as block:
            block.tensor(lambda e: S.run("tensor", e))
            block.vector(lambda e: S.run("vector", e))
            block.scalar(lambda e: S.run("scalar", e))
            block.sync(lambda e: S.run("sync", e))
            block.gpsimd(lambda e: S.run("gpsimd", e))

    return nc


_NC_CACHE = {}


def _get_nc(phases):
    key = tuple(sorted(phases))
    if key not in _NC_CACHE:
        _NC_CACHE[key] = build_program(phases)
    return _NC_CACHE[key]


def run_sharded(x_full, weights, phases=("ffn1", "attn", "ffn2")):
    n = 8
    nc = _get_nc(phases)
    wT = {k: np.ascontiguousarray(np.asarray(v, dtype=np.float32).T) for k, v in (
        ("wts", weights["L0_ffn_W_shared"]), ("wte", weights["L0_ffn_W_expert"]),
        ("wtq", weights["L0_attn_Wq"]), ("wtk", weights["L0_attn_Wk"]),
        ("wtv", weights["L0_attn_Wv"]), ("wtp", weights["L0_attn_Wp"]))}
    common = dict(wT)
    common["wg"] = np.ascontiguousarray(np.asarray(weights["L0_ffn_W_gate"], dtype=np.float32))
    for nm, key in (("b_q", "L0_attn_bq"), ("b_k", "L0_attn_bk"), ("b_v", "L0_attn_bv"),
                    ("b_p", "L0_attn_bp"), ("b_f", "L0_ffn_bias")):
        common[nm] = np.ascontiguousarray(np.asarray(weights[key], dtype=np.float32).reshape(1, D))
    x_full = np.asarray(x_full, dtype=np.float32)
    in_maps = []
    for i in range(n):
        m = dict(common)
        m["x"] = np.ascontiguousarray(x_full[2 * i:2 * i + 2].reshape(NSEQ * SEQ, D))
        in_maps.append(m)
    res = run_bass_kernel_spmd(nc, in_maps, core_ids=list(range(n)))
    outs = [np.asarray(r["out"]).reshape(NSEQ, SEQ, D) for r in res.results]
    return np.concatenate(outs, axis=0)


def kernel(**inputs):
    return run_sharded(inputs["L0_attn_x"], inputs)


# version 18
# speedup vs baseline: 1.1577x; 1.1577x over previous
import contextlib
import numpy as np
import concourse.bass as bass
import concourse.mybir as mybir
from concourse.bass_utils import run_bass_kernel_spmd

F32 = mybir.dt.float32
BF16 = mybir.dt.bfloat16
AF = mybir.ActivationFunctionType
ALU = mybir.AluOpType
AX = mybir.AxisListType

D = 1024
NH = 16
DH = 64
SEQ = 2048
NSEQ = 2
NCH = 4
NCHG = NSEQ * NCH
CH = 512
NT = 4
DCN = 8
HPN = 8
EPS = 1e-5

ENGINES = ("tensor", "vector", "scalar", "sync", "gpsimd")


class Sched:

    def __init__(self):
        self.ops = {e: [] for e in ENGINES}
        self.cnt = {}
        self.handles = {}

    def add_sem(self, name, handle):
        self.handles[name] = handle
        self.cnt[name] = 0

    CHAIN = {"vector": "s_dv", "scalar": "s_sc", "gpsimd": "s_gp", "tensor": "s_pe"}

    def op(self, eng, fn, wait=(), inc=None, k=1):
        n = None
        wait = list(wait)
        ch = self.CHAIN.get(eng)
        if ch is not None and k != 16:
            assert inc is None or inc == ch
            wait.append((ch, self.cnt[ch]))
            inc = ch
            k = 1
        if inc is not None:
            if k == 16:
                wait.append((inc, self.cnt[inc]))
            self.cnt[inc] += k
            n = self.cnt[inc]
        self.ops[eng].append((fn, tuple(wait), inc, k))
        return n

    def run(self, eng_name, eng):
        maxw = {}
        for fn, waits, inc, k in self.ops[eng_name]:
            for s, v in waits:
                if v > 0 and v > maxw.get(s, 0):
                    eng.wait_ge(self.handles[s], v)
                    maxw[s] = v
            instr = fn(eng)
            if inc is not None:
                if instr is None:
                    instr = eng.nop()
                instr.then_inc(self.handles[inc], k)


def build_program(phases=("ffn1", "attn", "ffn2")):
    do_ffn1 = "ffn1" in phases
    do_attn = "attn" in phases
    do_ffn2 = "ffn2" in phases

    nc = bass.Bass("TRN2", target_bir_lowering=False)

    x_in = nc.dram_tensor("x", [NSEQ * SEQ, D], F32, kind="ExternalInput").ap()
    out = nc.dram_tensor("out", [NSEQ * SEQ, D], F32, kind="ExternalOutput").ap()
    wts = nc.dram_tensor("wts", [D, D], F32, kind="ExternalInput").ap()
    wte = nc.dram_tensor("wte", [D, D], F32, kind="ExternalInput").ap()
    wtq = nc.dram_tensor("wtq", [D, D], F32, kind="ExternalInput").ap()
    wtk = nc.dram_tensor("wtk", [D, D], F32, kind="ExternalInput").ap()
    wtv = nc.dram_tensor("wtv", [D, D], F32, kind="ExternalInput").ap()
    wtp = nc.dram_tensor("wtp", [D, D], F32, kind="ExternalInput").ap()
    wg = nc.dram_tensor("wg", [D, 7], F32, kind="ExternalInput").ap()
    b_q = nc.dram_tensor("b_q", [1, D], F32, kind="ExternalInput").ap()
    b_k = nc.dram_tensor("b_k", [1, D], F32, kind="ExternalInput").ap()
    b_v = nc.dram_tensor("b_v", [1, D], F32, kind="ExternalInput").ap()
    b_p = nc.dram_tensor("b_p", [1, D], F32, kind="ExternalInput").ap()
    b_f = nc.dram_tensor("b_f", [1, D], F32, kind="ExternalInput").ap()
    wscr = nc.dram_tensor("wscr", [3, 2, 128, DCN, 512], BF16, kind="Internal").ap()

    S = Sched()

    with contextlib.ExitStack() as st:
        def sb(name, shape, dt):
            return st.enter_context(nc.sbuf_tensor(name, shape, dt))

        def mksem(name):
            S.add_sem(name, st.enter_context(nc.semaphore(name)))

        WS_T = sb("WS_T", [128, DCN, D], BF16)
        WE_T = sb("WE_T", [128, DCN, D], BF16)
        WP_T = sb("WP_T", [128, DCN, D], BF16)
        WSTR = sb("WSTR", [128, 2, DCN, 512], BF16)
        WG = sb("WG", [128, DCN, 7], BF16)
        BQR = sb("BQR", [1, D], BF16)
        BKR = sb("BKR", [1, D], BF16)
        BVR = sb("BVR", [1, D], BF16)
        BF2 = sb("BF2", [1, D], BF16)
        BPR = sb("BPR", [1, D], BF16)
        TRI = sb("TRI", [128, 128], BF16)
        IDE = sb("IDE", [128, 128], BF16)
        ONESR = sb("ONESR", [1, 512], BF16)
        ONESB = sb("ONESB", [65, 64], BF16)
        ONES16 = sb("ONES16", [16, 128], BF16)
        I16F = sb("I16F", [16, 16], F32)
        BD = sb("BD", [128, DCN, 16], BF16)

        for name in ("s_ld", "s_sw", "s_st", "s_of", "s_pe", "s_dv", "s_sc", "s_gp"):
            mksem(name)

        with nc.sbuf_tensor("STG", [128, DCN, D], F32) as STG, \
             nc.sbuf_tensor("STGB", [128, DCN, D], BF16) as STGB, \
             nc.sbuf_tensor("BSTG", [1, 6, D], F32) as BSTG, \
             nc.sbuf_tensor("WGS", [128, DCN, 7], F32) as WGS:

            def emit_prep():
                def gop(fn):
                    prev = S.cnt["s_gp"]
                    S.op("gpsimd", fn, wait=(("s_gp", prev),), inc="s_gp")
                gop(lambda g: g.memset(TRI[:, :], 0.0))
                gop(lambda g: g.affine_select(out=TRI[:, :], in_=TRI[:, :],
                                              compare_op=ALU.is_gt, fill=1.0, base=0,
                                              pattern=[[-1, 128]], channel_multiplier=1))
                gop(lambda g: g.memset(ONESR[:, :], 1.0))
                gop(lambda g: g.memset(ONESB[64:65, :], 1.0))
                gop(lambda g: g.memset(ONES16[:, :], 1.0))
                gop(lambda g: g.memset(IDE[:, :], 0.0))
                gop(lambda g: g.affine_select(out=IDE[:, :], in_=IDE[:, :],
                                              compare_op=ALU.not_equal, fill=1.0, base=0,
                                              pattern=[[-1, 128]], channel_multiplier=1))
                gop(lambda g: g.memset(I16F[:, :], 0.0))
                gop(lambda g: g.affine_select(out=I16F[:, :], in_=I16F[:, :],
                                              compare_op=ALU.not_equal, fill=1.0, base=0,
                                              pattern=[[-1, 16]], channel_multiplier=1))
                gop(lambda g: g.memset(BD[:, :, :], 0.0))

                def bdsets(g):
                    instr = None
                    for dc in range(DCN):
                        g.memset(BD[0:64, dc, 2 * dc:2 * dc + 1], 1.0)
                        instr = g.memset(BD[64:128, dc, 2 * dc + 1:2 * dc + 2], 1.0)
                    return instr
                gop(bdsets)

                S.op("sync", lambda e: e.dma_start(out=BSTG[0:1, 0, :], in_=b_q), inc="s_ld", k=16)
                S.op("sync", lambda e: e.dma_start(out=BSTG[0:1, 1, :], in_=b_k), inc="s_ld", k=16)
                S.op("sync", lambda e: e.dma_start(out=BSTG[0:1, 2, :], in_=b_v), inc="s_ld", k=16)
                S.op("sync", lambda e: e.dma_start(out=BSTG[0:1, 3, :], in_=b_f), inc="s_ld", k=16)
                S.op("sync", lambda e: e.dma_start(out=BSTG[0:1, 4, :], in_=b_p), inc="s_ld", k=16)
                bias_dma = S.op("sync", lambda e: e.dma_start(
                    out=WGS[:, :, :], in_=wg.rearrange("(c p) g -> p c g", p=128)), inc="s_ld", k=16)
                S.op("vector", lambda e: e.tensor_scalar_mul(BQR[0:1, :], BSTG[0:1, 0, :], 2.0 ** -1.5),
                     wait=(("s_ld", bias_dma),))
                S.op("vector", lambda e: e.tensor_scalar_mul(BKR[0:1, :], BSTG[0:1, 1, :], 2.0 ** -1.5))
                S.op("vector", lambda e: e.tensor_copy(BVR[0:1, :], BSTG[0:1, 2, :]))
                S.op("vector", lambda e: e.tensor_scalar_mul(BF2[:, :], BSTG[0:1, 3, :], 0.5))
                S.op("vector", lambda e: e.tensor_copy(BPR[:, :], BSTG[0:1, 4, :]))
                S.op("vector", lambda e: e.tensor_copy(WG[:, :, :], WGS[:, :, :]), inc="s_dv")

                RS8 = 2.0 ** -1.5
                plan = [(wts, WS_T, 0.5, None), (wte, WE_T, 0.5, None),
                        (wtp, WP_T, 1.0, None), (wtq, None, RS8, 0),
                        (wtk, None, RS8, 1), (wtv, None, 1.0, 2)]
                prev_cast = 0
                prev_store = 0
                for i, (src, dst, scale, scr_i) in enumerate(plan):
                    ld = S.op("sync", lambda e, s=src: e.dma_start(
                        out=STG[:, :, :], in_=s.rearrange("(c p) o -> p c o", p=128)),
                        wait=(("s_dv", prev_cast),), inc="s_ld", k=16)
                    if dst is not None:
                        prev_cast = S.op("vector", lambda e, d=dst, sc=scale: e.tensor_scalar_mul(
                            d[:, :, :], STG[:, :, :], sc), wait=(("s_ld", ld),), inc="s_dv")
                    else:
                        prev_cast = S.op("vector", lambda e, sc=scale: e.tensor_scalar_mul(
                            STGB[:, :, :], STG[:, :, :], sc),
                            wait=(("s_ld", ld), ("s_ld", prev_store)), inc="s_dv")
                        for half in range(2):
                            prev_store = S.op("sync", lambda e, j=scr_i, h=half: e.dma_start(
                                out=wscr[j, h], in_=STGB[:, :, h * 512:(h + 1) * 512]),
                                wait=(("s_dv", prev_cast),), inc="s_ld", k=16)
                return S.cnt["s_dv"], S.cnt["s_ld"]

            prep_dv, prep_ld = emit_prep()

        KT = sb("KT", [128, HPN, SEQ], BF16)
        VB = sb("VB", [128, 16, NH, 65], BF16)
        QT = sb("QT", [128, HPN, CH], BF16)
        XC = sb("XC", [128, NT, D], F32)
        XN = sb("XN", [128, 3, D], BF16)
        XNT = sb("XNT", [128, DCN, CH], BF16)
        OT = sb("OT", [128, HPN, CH], BF16)
        OTMP = sb("OTMP", [64, 512], BF16)
        PT = sb("PT", [128, 8, 512], BF16)
        JUNK = sb("JUNK", [128, 512], F32)
        SQS = sb("SQS", [128, 2, 512], BF16)
        RBB = sb("RBB", [64, 512], F32)
        SGR = sb("SGR", [65, 512], F32)
        RGB = sb("RGB", [65, 2, 512], BF16)
        BNS = sb("BNS", [128, NT, 12], F32)
        STAT = sb("STAT", [128, 6, NT], F32)
        SMW = sb("SMW", [128, NT, 4], F32)
        GD = sb("GD", [128, NT, 8], F32)
        GEX = sb("GEX", [128, NT, 8], F32)
        GW = sb("GW", [128, 13], F32)
        GC = sb("GC", [128, 8], F32)
        GGATE = sb("GGATE", [128, NT], F32)
        NRM = sb("NRM", [16, 8], F32)
        D16 = sb("D16", [16, 16], BF16)
        MBC = sb("MBC", [128, 16], F32)
        MBCC = sb("MBCC", [128, 1], F32)

        PSA = st.enter_context(nc.psum_tensor("PSA", [128, 4096], F32))

        def bank(b, w=512, off=0):
            return PSA[:, b * 512 + off: b * 512 + off + w]

        S.op("gpsimd", lambda e: e.memset(VB[:, :, :, 64:65], 1.0),
             wait=(("s_dv", prep_dv), ("s_ld", prep_ld), ("s_gp", S.cnt["s_gp"])), inc="s_gp")
        gp_done = S.cnt["s_gp"]

        load_ms = {}
        store_ms = {}
        trans_ms = {}
        xnt_free = [0]
        xn_free = {}
        xn_rot = [0]
        tslot_free = {0: 0, 1: 0}

        for t in range(NT):
            load_ms[(0, t)] = S.op("sync", lambda e, t=t: e.dma_start(
                out=XC[:, t, :], in_=x_in[t * 128:(t + 1) * 128, :]),
                wait=(("s_dv", prep_dv), ("s_gp", gp_done)) if t == 0 else (),
                inc="s_ld", k=16)

        def make_ln(g, ln):
            def stats(t):
                w = []
                if ln == 0:
                    w.append(("s_ld", load_ms[(g, t)]))
                S.op("vector", lambda e, t=t: e.bn_stats(BNS[:, t, 0:6], XC[:, t, 0:512]), wait=w)
                S.op("vector", lambda e, t=t: e.bn_stats(BNS[:, t, 6:12], XC[:, t, 512:1024]))
                mvt = BNS[:, t, 1:12:3]
                vvt = BNS[:, t, 2:12:3]
                S.op("vector", lambda e, t=t, mvt=mvt: e.tensor_mul(SMW[:, t, :], mvt, mvt))
                S.op("vector", lambda e, t=t, vvt=vvt: e.scalar_tensor_tensor(
                    SMW[:, t, :], SMW[:, t, :], 256.0, vvt, ALU.mult, ALU.add))
                S.op("vector", lambda e, t=t: e.tensor_reduce(
                    STAT[:, 5, t:t + 1], SMW[:, t, :], axis=AX.X, op=ALU.add))
                S.op("vector", lambda e, t=t, mvt=mvt: e.tensor_reduce(
                    STAT[:, 0, t:t + 1], mvt, axis=AX.X, op=ALU.add))
                S.op("vector", lambda e, t=t: e.tensor_scalar_mul(
                    STAT[:, 0, t:t + 1], STAT[:, 0, t:t + 1], 0.25))
                S.op("vector", lambda e, t=t: e.tensor_mul(
                    STAT[:, 1, t:t + 1], STAT[:, 0, t:t + 1], STAT[:, 0, t:t + 1]))
                S.op("vector", lambda e, t=t: e.scalar_tensor_tensor(
                    STAT[:, 2, t:t + 1], STAT[:, 5, t:t + 1], 1.0 / 1024.0,
                    STAT[:, 1, t:t + 1], ALU.mult, ALU.subtract))
                return S.op("vector", lambda e, t=t: e.tensor_scalar(
                    STAT[:, 2, t:t + 1], STAT[:, 2, t:t + 1], EPS, None, ALU.add))

            def lnexp(t, dvv):
                S.op("scalar", lambda e, t=t: e.activation(
                    STAT[:, 3, t:t + 1], STAT[:, 2, t:t + 1], AF.Ln),
                    wait=(("s_dv", dvv),))
                return S.op("scalar", lambda e, t=t: e.activation(
                    STAT[:, 4, t:t + 1], STAT[:, 3, t:t + 1], AF.Exp, scale=-0.5))

            slot_of_t = {}

            def apply_t(t, scm):
                slot = xn_rot[0] % 3
                xn_rot[0] += 1
                slot_of_t[t] = slot
                w = [("s_sc", scm)] + list(xn_free.get(slot, ()))
                ap_ms = S.op("vector", lambda e, t=t, slot=slot: e.tensor_scalar(
                    XN[:, slot, :], XC[:, t, :], STAT[:, 0, t:t + 1], STAT[:, 4, t:t + 1],
                    ALU.subtract, ALU.mult), wait=w)
                return ap_ms

            def transpose_t(t, ap_ms):
                slot = slot_of_t[t]
                tb = (6 + t % 2) * 512
                last_pe = 0
                for dc in range(DCN):
                    pv = PSA[:, tb + dc * 64:tb + dc * 64 + 64].bitcast(BF16)
                    last_pe = S.op("tensor", lambda e, slot=slot, dc=dc, pv=pv: e.transpose(
                        pv, XN[:, slot, dc * 128:(dc + 1) * 128], IDE[:, :]),
                        wait=(("s_dv", ap_ms), ("s_dv", tslot_free[t % 2])), inc="s_pe")
                cp = S.op("vector", lambda e, t=t, tb=tb: e.tensor_copy(
                    XNT[:, :, t * 128:(t + 1) * 128],
                    PSA[:, tb:tb + 512].bitcast(BF16).rearrange(
                        "p (c f) -> p c f", c=DCN)),
                    wait=(("s_pe", last_pe), ("s_pe", xnt_free[0])))
                tslot_free[t % 2] = cp
                trans_ms[(g, ln, t)] = (("s_dv", cp),)
                xn_free[slot] = (("s_pe", last_pe),)
                return cp

            exps = {}
            applied = {}

            def pre(t):
                if t not in exps:
                    exps[t] = lnexp(t, stats(t))

            def full(t):
                if t in applied:
                    return
                pre(t)
                applied[t] = True
                transpose_t(t, apply_t(t, exps[t]))

            return {"pre": pre, "full": full, "exps": exps, "applied": applied,
                    "apply": apply_t, "transpose": transpose_t}

        def emit_ffn(g, ln, entry_dv, lnh, tail=None):
            evac_ms = {}
            gate_ms = {}
            mm_ms = {}
            for t in range(NT):
                lnh["pre"](t)
            ap = {}
            for t in range(3):
                if t not in lnh["applied"]:
                    lnh["applied"][t] = True
                    ap[t] = lnh["apply"](t, lnh["exps"][t])
            if 0 in ap:
                lnh["transpose"](0, ap[0])
            for u, (t, oh) in enumerate([(t, oh) for t in range(NT) for oh in range(2)]):
                slot = u % 2
                bs, be = slot * 2, slot * 2 + 1
                gb = 4 + (t % 2)
                common = list(trans_ms[(g, ln, t)])
                if u >= 2:
                    common.append(("s_dv", evac_ms[u - 2]))
                else:
                    common.append(("s_dv", entry_dv))
                if oh == 0 and t >= 2:
                    common.append(gate_ms[t - 2])

                def mms(e, t=t, oh=oh, bs=bs, be=be, gb=gb):
                    tl = slice(t * 128, (t + 1) * 128)
                    ohs = slice(oh * 512, (oh + 1) * 512)
                    for dc in range(DCN):
                        e.matmul(bank(bs), XNT[:, dc, tl], WS_T[:, dc, ohs],
                                 start=(dc == 0), stop=False)
                    e.matmul(bank(bs), ONESR[0:1, 0:128], BF2[0:1, ohs],
                             start=False, stop=True)
                    for dc in range(DCN):
                        instr = e.matmul(bank(be), XNT[:, dc, tl], WE_T[:, dc, ohs],
                                         start=(dc == 0), stop=(dc == DCN - 1))
                    if oh == 0:
                        for dc in range(DCN):
                            instr = e.matmul(bank(gb, 7, (t // 2) * 8), XNT[:, dc, tl],
                                             WG[:, dc, :], start=(dc == 0),
                                             stop=(dc == DCN - 1))
                    return instr
                mm_ms[u] = S.op("tensor", mms, wait=common, inc="s_pe")

                if oh == 0 and t + 1 < NT:
                    if (t + 1) in ap:
                        lnh["transpose"](t + 1, ap[t + 1])
                    else:
                        lnh["full"](t + 1)
                    if t == 0 and 3 not in lnh["applied"]:
                        lnh["applied"][3] = True
                        ap[3] = lnh["apply"](3, lnh["exps"][3])

                if oh == 0:
                    gv = bank(gb, 7, (t // 2) * 8)
                    S.op("vector", lambda e, gv=gv: e.tensor_reduce(
                        GC[:, 0:1], gv, axis=AX.X, op=ALU.max), wait=(("s_pe", mm_ms[u]),))
                    d1 = S.op("vector", lambda e, gv=gv, t=t: e.tensor_scalar(
                        GD[:, t, 0:7], gv, GC[:, 0:1], None, ALU.subtract))
                    s1 = S.op("scalar", lambda e, t=t: e.activation(
                        GEX[:, t, 0:7], GD[:, t, 0:7], AF.Exp), wait=(("s_dv", d1),), inc="s_sc")
                    S.op("vector", lambda e, t=t: e.tensor_reduce(
                        GC[:, 1:2], GEX[:, t, 0:7], axis=AX.X, op=ALU.add), wait=(("s_sc", s1),))
                    S.op("vector", lambda e, t=t: e.tensor_single_scalar(
                        GW[:, 0:7], GD[:, t, 0:7], 0.0, ALU.is_equal))
                    S.op("vector", lambda e, t=t: e.tensor_sub(GW[:, 0:7], GEX[:, t, 0:7], GW[:, 0:7]))
                    S.op("vector", lambda e: e.tensor_reduce(
                        GC[:, 2:3], GW[:, 0:7], axis=AX.X, op=ALU.max))
                    S.op("vector", lambda e: e.tensor_scalar(
                        GC[:, 3:4], GC[:, 2:3], 1.0, None, ALU.add))
                    S.op("vector", lambda e: e.reciprocal(GC[:, 5:6], GC[:, 1:2]))
                    S.op("vector", lambda e: e.tensor_mul(GC[:, 4:5], GC[:, 3:4], GC[:, 5:6]))
                    S.op("vector", lambda e: e.reciprocal(GC[:, 6:7], GC[:, 4:5]))
                    gate_ms[t] = ("s_dv", S.op("vector", lambda e, t=t: e.tensor_mul(
                        GGATE[:, t:t + 1], GC[:, 4:5], GC[:, 6:7])))

                ohs = slice(oh * 512, (oh + 1) * 512)
                S.op("vector", lambda e, t=t, be=be, ohs=ohs: e.scalar_tensor_tensor(
                    JUNK[:, :], bank(be), GGATE[:, t:t + 1], XC[:, t, ohs],
                    ALU.mult, ALU.add), wait=(("s_pe", mm_ms[u]), gate_ms[t]))
                evac_ms[u] = S.op("vector", lambda e, t=t, bs=bs, ohs=ohs: e.scalar_tensor_tensor(
                    XC[:, t, ohs], bank(bs), 1.0, JUNK[:, :], ALU.mult, ALU.add))
                if oh == 1 and tail is not None:
                    tail(t, evac_ms[u])
            xnt_free[0] = S.cnt["s_pe"]
            return {t: evac_ms[2 * t + 1] for t in range(NT)}

        def emit_qkv(g, slot_free, entry_dv):
            c = g % NCH
            ld = {}

            def stream(i, widx, half, wait_pe):
                ld[i] = S.op("scalar", lambda e, widx=widx, half=half, i=i: e.dma_start(
                    out=WSTR[:, i % 2, :, :], in_=wscr[widx, half]),
                    wait=(("s_pe", wait_pe), ("s_ld", prep_ld)), inc="s_sw", k=16)

            stream(0, 0, 0, slot_free[0])
            stream(1, 0, 1, slot_free[1])

            qmm, qev = {}, {}
            for hp in range(HPN):
                b = hp % 2
                w = [("s_sw", ld[0 if hp < 4 else 1])] + list(trans_ms[(g, 1, "all")])
                if hp >= 2:
                    w.append(qev[hp - 2])
                else:
                    w.append(("s_dv", entry_dv))

                def qm(e, hp=hp, b=b):
                    sl = 0 if hp < 4 else 1
                    col = slice((hp % 4) * 128, (hp % 4 + 1) * 128)
                    for dc in range(DCN):
                        e.matmul(bank(b), WSTR[:, sl, dc, col], XNT[:, dc, :],
                                 start=(dc == 0), stop=False)
                    return e.matmul(bank(b), BQR[0:1, hp * 128:(hp + 1) * 128],
                                    ONESR[0:1, 0:512], start=False, stop=True)
                qmm[hp] = S.op("tensor", qm, wait=w, inc="s_pe")
                qev[hp] = ("s_dv", S.op("vector", lambda e, hp=hp, b=b: e.tensor_copy(
                    QT[:, hp, :], bank(b)), wait=(("s_pe", qmm[hp]),)))

            stream(2, 1, 0, qmm[3])
            stream(3, 1, 1, qmm[7])

            kmm, kev = {}, {}
            for hp in range(HPN):
                b = 2 + hp % 2
                w = [("s_sw", ld[2 if hp < 4 else 3])]
                if hp >= 2:
                    w.append(kev[hp - 2])

                def km(e, hp=hp, b=b):
                    sl = 0 if hp < 4 else 1
                    col = slice((hp % 4) * 128, (hp % 4 + 1) * 128)
                    for dc in range(DCN):
                        e.matmul(bank(b), WSTR[:, sl, dc, col], XNT[:, dc, :],
                                 start=(dc == 0), stop=False)
                    return e.matmul(bank(b), BKR[0:1, hp * 128:(hp + 1) * 128],
                                    ONESR[0:1, 0:512], start=False, stop=True)
                kmm[hp] = S.op("tensor", km, wait=w, inc="s_pe")
                kev[hp] = ("s_dv", S.op("vector", lambda e, hp=hp, b=b, c=c: e.tensor_copy(
                    KT[:, hp, c * 512:(c + 1) * 512], bank(b)),
                    wait=(("s_pe", kmm[hp]),)))

            stream(4, 2, 0, kmm[3])
            stream(5, 2, 1, kmm[7])

            vmm, vev = {}, {}
            for t in range(NT):
                pb = 4 + 2 * (t % 2)
                w = [("s_sw", ld[5])]
                if t >= 2:
                    w.append(vev[t - 2])

                def vm(e, t=t, pb=pb):
                    tl = slice(t * 128, (t + 1) * 128)
                    instr = None
                    for half in range(2):
                        for dc in range(DCN):
                            e.matmul(bank(pb + half), XNT[:, dc, tl], WSTR[:, half, dc, :],
                                     start=(dc == 0), stop=False)
                        instr = e.matmul(bank(pb + half), ONESR[0:1, 0:128],
                                         BVR[0:1, half * 512:(half + 1) * 512],
                                         start=False, stop=True)
                    return instr
                vmm[t] = S.op("tensor", vm, wait=w, inc="s_pe")
                jt = c * 4 + t
                vev[t] = ("s_dv", S.op("vector", lambda e, jt=jt, pb=pb: e.tensor_copy(
                    VB[:, jt, :, 0:64],
                    PSA[:, pb * 512:(pb + 2) * 512].rearrange("p (h v) -> p h v", h=16)),
                    wait=(("s_pe", vmm[t]),)))

            slot_free[0] = vmm[NT - 1]
            slot_free[1] = vmm[NT - 1]
            xnt_free[0] = S.cnt["s_pe"]
            return qev, kev, vev

        def emit_norms(g, qev, kev, vev):
            c = g % NCH
            nmm = {}
            sq_ms = {}
            for i, which in enumerate(("q", "k")):
                for hp in range(HPN):
                    gidx = i * 8 + hp
                    w = [qev[hp] if which == "q" else kev[hp]]
                    if gidx >= 2:
                        pi, php = divmod(gidx - 2, 8)
                        w.append(("s_pe", nmm[(pi, php)]))
                    src = (lambda hp=hp: QT[:, hp, :]) if which == "q" else \
                          (lambda hp=hp, c=c: KT[:, hp, c * 512:(c + 1) * 512])
                    sq_ms[(i, hp)] = S.op("vector", lambda e, hp=hp, src=src: e.tensor_mul(
                        SQS[:, hp % 2, :], src(), src()), wait=w)
                    wpe = [("s_dv", sq_ms[(i, hp)])]
                    if i == 0 and hp < 2:
                        wpe.append(vev[NT - 1])
                    nmm[(i, hp)] = S.op("tensor", lambda e, i=i, hp=hp: e.matmul(
                        PSA[0:16, i * 512:(i + 1) * 512], BD[:, hp, :], SQS[:, hp % 2, :],
                        start=(hp == 0), stop=(hp == HPN - 1)), wait=wpe, inc="s_pe")
            S.op("vector", lambda e: e.reduce_max(NRM[:, 0:1], PSA[0:16, 0:512], axis=AX.X),
                 wait=(("s_pe", nmm[(1, HPN - 1)]),))
            S.op("vector", lambda e: e.reduce_max(NRM[:, 1:2], PSA[0:16, 512:1024], axis=AX.X))
            if c == 0:
                S.op("vector", lambda e: e.tensor_copy(NRM[:, 2:3], NRM[:, 1:2]))
            else:
                S.op("vector", lambda e, c=c: e.tensor_tensor(
                    NRM[:, 2 + c % 2:3 + c % 2], NRM[:, 2 + (c + 1) % 2:3 + (c + 1) % 2],
                    NRM[:, 1:2], ALU.max))
            S.op("vector", lambda e, c=c: e.tensor_add(NRM[:, 4:5], NRM[:, 0:1],
                                                       NRM[:, 2 + c % 2:3 + c % 2]))
            dvm = S.op("vector", lambda e: e.tensor_scalar(
                D16[:, :], I16F[:, :], NRM[:, 4:5], -0.55, ALU.mult, ALU.mult), inc="s_dv")
            mh = S.op("tensor", lambda e: e.matmul(
                PSA[:, 1024:1040], ONES16[:, :], D16[:, :], start=True, stop=True),
                wait=(("s_dv", dvm),), inc="s_pe")
            cp = S.op("scalar", lambda e: e.copy(MBC[:, :], PSA[:, 1024:1040]),
                      wait=(("s_pe", mh),), inc="s_sc")
            mbc = S.op("vector", lambda e: e.tensor_reduce(
                MBCC[:, :], MBC[:, :], axis=AX.X, op=ALU.min),
                wait=(("s_sc", cp),))
            return mbc

        def emit_attention(g, mbc_ms, attn_state):
            c = g % NCH
            kend = 4 * (c + 1)
            JP = kend // 2
            sb_free = attn_state["sb_free"]
            pt_last_pv = attn_state["pt_last_pv"]
            o_free_dv = attn_state["o_free_dv"]
            r_free_sc = attn_state["r_free_sc"]
            of_last = attn_state["of_last"]
            pend = attn_state["pend"]

            def off_of(j):
                return (j - 4 * c) * 128 if j >= 4 * c else 0

            def ob(h):
                return 4 + h % 2

            def rb(h):
                return 6 + h % 2

            def emit_ss(he, ho, jp):
                hp = he // 2
                w = [("s_sc", sb_free[0]), ("s_dv", mbc_ms)]

                def fn(e, hp=hp, jp=jp):
                    instr = None
                    for h in (he, ho):
                        hb = 64 * (h % 2)
                        for j in (2 * jp, 2 * jp + 1):
                            off = off_of(j)
                            b = 2 * (h % 2) + (j % 2)
                            instr = e.matmul(bank(b, 512 - off, off),
                                             KT[hb:hb + 64, hp, j * 128:(j + 1) * 128],
                                             QT[hb:hb + 64, hp, off:512],
                                             start=True, stop=True)
                    return instr
                return S.op("tensor", fn, wait=w, inc="s_pe")

            def emit_expb(jp, s_ms):
                w = [("s_pe", s_ms), ("s_pe", pt_last_pv.get(jp % 2, 0))]
                o = PT.rearrange("p (a b) f -> p a b f", a=2)[:, :, 2 * (jp % 2):2 * (jp % 2) + 2, :]
                i = PSA[:, 0:2048].rearrange("p (a b c) -> p a b c", a=2, b=2)
                return S.op("scalar", lambda e, o=o, i=i: e.activation(
                    o, i, AF.Exp, bias=MBCC[:, :]), wait=w, inc="s_sc")

            def emit_masks(h, jp, e_ms):
                ms = 0
                for j in (2 * jp, 2 * jp + 1):
                    if j >= 4 * c:
                        off = off_of(j)
                        sl = 4 * (h % 2) + (j % 4)
                        ms = S.op("vector", lambda e, sl=sl, off=off: e.tensor_mul(
                            PT[:, sl, off:off + 128], PT[:, sl, off:off + 128], TRI[:, :]),
                            wait=(("s_sc", e_ms),))
                return ms

            def emit_pvv(he, ho, jp, ee, eo, me, mo):
                w = [("s_sc", ee), ("s_sc", eo)]
                if me:
                    w.append(("s_dv", me))
                if mo:
                    w.append(("s_dv", mo))
                if jp == 0:
                    w.append(("s_dv", o_free_dv[0]))
                    w.append(("s_dv", o_free_dv[1]))

                def fn(e, he=he, ho=ho, jp=jp):
                    instr = None
                    for h in (he, ho):
                        for j in (2 * jp, 2 * jp + 1):
                            off = off_of(j)
                            sl = 4 * (h % 2) + (j % 4)
                            instr = e.matmul(
                                PSA[0:65, ob(h) * 512 + off:(ob(h) + 1) * 512],
                                VB[:, j, h, :], PT[:, sl, off:512],
                                start=(j == 0), stop=(j == kend - 1))
                    return instr
                pv = S.op("tensor", fn, wait=w, inc="s_pe")
                pt_last_pv[jp % 2] = pv
                return pv

            def emit_sg_chain(h, pv_ms):
                S.op("scalar", lambda e, h=h: e.activation(
                    SGR[64:65, 0:512], PSA[64:65, ob(h) * 512:(ob(h) + 1) * 512], AF.Ln),
                    wait=(("s_pe", pv_ms),))
                rgb = S.op("scalar", lambda e, h=h: e.activation(
                    RGB[64:65, h % 2, :], SGR[64:65, 0:512], AF.Exp, scale=-1.0))
                return {"h": h, "rgb_sc": rgb}

            def emit_finalize(p, rbb_free_dv):
                h = p["h"]
                r1 = S.op("tensor", lambda e, h=h: e.matmul(
                    PSA[0:64, rb(h) * 512:(rb(h) + 1) * 512], ONESB[64:65, :],
                    RGB[64:65, h % 2, :], start=True, stop=True),
                    wait=(("s_sc", p["rgb_sc"]), r_free_sc[h % 2]), inc="s_pe")
                rbc = S.op("vector", lambda e, h=h: e.tensor_copy(
                    RBB[:, :], PSA[0:64, rb(h) * 512:(rb(h) + 1) * 512]),
                    wait=(("s_pe", r1), ("s_dv", rbb_free_dv)))
                r_free_sc[h % 2] = ("s_dv", rbc)
                hp, hb = h // 2, 64 * (h % 2)
                if hb == 0:
                    ot = S.op("vector", lambda e, hp=hp, h=h: e.tensor_tensor(
                        OT[0:64, hp, :], PSA[0:64, ob(h) * 512:(ob(h) + 1) * 512],
                        RBB[:, :], ALU.mult))
                else:
                    ot = S.op("vector", lambda e, h=h: e.tensor_tensor(
                        OTMP[:, :], PSA[0:64, ob(h) * 512:(ob(h) + 1) * 512],
                        RBB[:, :], ALU.mult),
                        wait=(("s_of", of_last[0]),))
                    of_last[0] = S.op("sync", lambda e, hp=hp: e.dma_start(
                        out=OT[64:128, hp, :], in_=OTMP[:, :]),
                        wait=(("s_dv", ot),), inc="s_of", k=16)
                o_free_dv[h % 2] = ot
                return ot

            ot_last = 0
            for hp2 in range(HPN):
                he, ho = 2 * hp2, 2 * hp2 + 1
                s_ms = emit_ss(he, ho, 0)
                if pend[0] is not None:
                    pe_, po_ = pend[0]
                    o1 = emit_finalize(pe_, 0)
                    ot_last = emit_finalize(po_, o1)
                    pend[0] = None
                ee = emit_expb(0, s_ms)
                me = emit_masks(he, 0, ee)
                mo = emit_masks(ho, 0, ee)
                sb_free[0] = ee
                prev = (0, ee, ee, me, mo)
                for jp in range(1, JP):
                    s_ms = emit_ss(he, ho, jp)
                    ee = emit_expb(jp, s_ms)
                    me = emit_masks(he, jp, ee)
                    mo = emit_masks(ho, jp, ee)
                    sb_free[0] = ee
                    pjp, pee, peo, pme, pmo = prev
                    emit_pvv(he, ho, pjp, pee, peo, pme, pmo)
                    prev = (jp, ee, ee, me, mo)
                pjp, pee, peo, pme, pmo = prev
                pv = emit_pvv(he, ho, pjp, pee, peo, pme, pmo)
                pend[0] = (emit_sg_chain(he, pv), emit_sg_chain(ho, pv))
            pe_, po_ = pend[0]
            o1 = emit_finalize(pe_, 0)
            ot_last = emit_finalize(po_, o1)
            pend[0] = None
            return ot_last

        def emit_delta(g, ot_dv, tail=None):
            ev = {}
            sc_now = S.cnt["s_sc"]
            of_now = S.cnt["s_of"]
            for t in range(NT):
                pb = 2 * (t % 2)
                w = [("s_dv", ot_dv), ("s_of", of_now)]
                if t >= 2:
                    w.append(("s_dv", ev[t - 2]))
                else:
                    w.append(("s_sc", sc_now))

                def dm(e, t=t, pb=pb):
                    tl = slice(t * 128, (t + 1) * 128)
                    instr = None
                    for half in range(2):
                        for hp in range(DCN):
                            e.matmul(bank(pb + half), OT[:, hp, tl],
                                     WP_T[:, hp, half * 512:(half + 1) * 512],
                                     start=(hp == 0), stop=False)
                        instr = e.matmul(bank(pb + half), ONESR[0:1, 0:128],
                                         BPR[0:1, half * 512:(half + 1) * 512],
                                         start=False, stop=True)
                    return instr
                m = S.op("tensor", dm, wait=w, inc="s_pe")

                S.op("vector", lambda e, t=t, pb=pb: e.tensor_add(
                    XC[:, t, 0:512], XC[:, t, 0:512], bank(pb)), wait=(("s_pe", m),))
                ev[t] = S.op("vector", lambda e, t=t, pb=pb: e.tensor_add(
                    XC[:, t, 512:1024], XC[:, t, 512:1024], bank(pb + 1)))
                if tail is not None:
                    tail(t, ev[t])
            return ev

        def store_load_t(g, t, done_ms):
            base = g * CH + t * 128
            store_ms[(g, t)] = S.op("sync", lambda e, base=base, t=t: e.dma_start(
                out=out[base:base + 128, :], in_=XC[:, t, :]),
                wait=(("s_dv", done_ms),), inc="s_st", k=16)
            if g + 1 < NCHG:
                nbase = (g + 1) * CH + t * 128
                load_ms[(g + 1, t)] = S.op("sync", lambda e, nbase=nbase, t=t: e.dma_start(
                    out=XC[:, t, :], in_=x_in[nbase:nbase + 128, :]),
                    wait=(("s_st", store_ms[(g, t)]),), inc="s_ld", k=16)

        slot_free = [0, 0]
        attn_state = {"sb_free": {0: 0, 1: 0},
                      "pt_last_pv": {},
                      "o_free_dv": {0: 0, 1: 0},
                      "r_free_sc": {0: ("s_dv", 0), 1: ("s_dv", 0)},
                      "of_last": [0], "pend": [None]}
        LNH = {}

        def get_lnh(g, ln):
            if (g, ln) not in LNH:
                LNH[(g, ln)] = make_ln(g, ln)
            return LNH[(g, ln)]

        ln_next = {0: (1 if do_attn else (2 if do_ffn2 else None)),
                   1: (2 if do_ffn2 else None), 2: None}

        def mk_tail(g, ln):
            nxt = ln_next[ln]
            if nxt is not None:
                return lambda t, dv_ms: get_lnh(g, nxt)["pre"](t)

            def last_tail(t, dv_ms):
                store_load_t(g, t, dv_ms)
                if g + 1 < NCHG:
                    get_lnh(g + 1, 0)["pre"](t)
            return last_tail

        for g in range(NCHG):
            done = None
            guard = S.cnt["s_dv"]
            if do_ffn1:
                done = emit_ffn(g, 0, guard, get_lnh(g, 0), mk_tail(g, 0))
            elif do_attn or do_ffn2:
                pass
            if do_attn:
                guard = S.cnt["s_dv"]
                lnh2 = get_lnh(g, 1)
                for t in range(NT):
                    lnh2["full"](t)
                trans_ms[(g, 1, "all")] = tuple(
                    trans_ms[(g, 1, t)][0] for t in range(NT))
                qev, kev, vev = emit_qkv(g, slot_free, guard)
                mbc = emit_norms(g, qev, kev, vev)
                ot_dv = emit_attention(g, mbc, attn_state)
                done = emit_delta(g, ot_dv, mk_tail(g, 1))
            if do_ffn2:
                guard = S.cnt["s_dv"]
                done = emit_ffn(g, 2, guard, get_lnh(g, 2), mk_tail(g, 2))
            if done is None:
                done = {t: S.cnt["s_dv"] for t in range(NT)}
            for t in range(NT):
                if (g, t) not in store_ms:
                    store_load_t(g, t, done[t] if isinstance(done, dict) else S.cnt["s_dv"])
                    if g + 1 < NCHG:
                        get_lnh(g + 1, 0)["pre"](t)

        fw = [("s_st", S.cnt["s_st"]), ("s_ld", S.cnt["s_ld"]),
              ("s_sw", S.cnt["s_sw"]), ("s_of", S.cnt["s_of"])]
        S.op("sync", lambda e: e.nop(), wait=tuple(fw))

        with nc.Block() as block:
            block.tensor(lambda e: S.run("tensor", e))
            block.vector(lambda e: S.run("vector", e))
            block.scalar(lambda e: S.run("scalar", e))
            block.sync(lambda e: S.run("sync", e))
            block.gpsimd(lambda e: S.run("gpsimd", e))

    return nc


_NC_CACHE = {}


def _get_nc(phases):
    key = tuple(sorted(phases))
    if key not in _NC_CACHE:
        _NC_CACHE[key] = build_program(phases)
    return _NC_CACHE[key]


def run_sharded(x_full, weights, phases=("ffn1", "attn", "ffn2")):
    n = 8
    nc = _get_nc(phases)
    wT = {k: np.ascontiguousarray(np.asarray(v, dtype=np.float32).T) for k, v in (
        ("wts", weights["L0_ffn_W_shared"]), ("wte", weights["L0_ffn_W_expert"]),
        ("wtq", weights["L0_attn_Wq"]), ("wtk", weights["L0_attn_Wk"]),
        ("wtv", weights["L0_attn_Wv"]), ("wtp", weights["L0_attn_Wp"]))}
    common = dict(wT)
    common["wg"] = np.ascontiguousarray(np.asarray(weights["L0_ffn_W_gate"], dtype=np.float32))
    for nm, key in (("b_q", "L0_attn_bq"), ("b_k", "L0_attn_bk"), ("b_v", "L0_attn_bv"),
                    ("b_p", "L0_attn_bp"), ("b_f", "L0_ffn_bias")):
        common[nm] = np.ascontiguousarray(np.asarray(weights[key], dtype=np.float32).reshape(1, D))
    x_full = np.asarray(x_full, dtype=np.float32)
    in_maps = []
    for i in range(n):
        m = dict(common)
        m["x"] = np.ascontiguousarray(x_full[2 * i:2 * i + 2].reshape(NSEQ * SEQ, D))
        in_maps.append(m)
    res = run_bass_kernel_spmd(nc, in_maps, core_ids=list(range(n)))
    outs = [np.asarray(r["out"]).reshape(NSEQ, SEQ, D) for r in res.results]
    return np.concatenate(outs, axis=0)


def kernel(**inputs):
    return run_sharded(inputs["L0_attn_x"], inputs)


# version 19
# speedup vs baseline: 1.1816x; 1.0207x over previous
import contextlib
import numpy as np
import concourse.bass as bass
import concourse.mybir as mybir
from concourse.bass_utils import run_bass_kernel_spmd

F32 = mybir.dt.float32
BF16 = mybir.dt.bfloat16
AF = mybir.ActivationFunctionType
ALU = mybir.AluOpType
AX = mybir.AxisListType

D = 1024
NH = 16
DH = 64
SEQ = 2048
NSEQ = 2
NCH = 4
NCHG = NSEQ * NCH
CH = 512
NT = 4
DCN = 8
HPN = 8
EPS = 1e-5

ENGINES = ("tensor", "vector", "scalar", "sync", "gpsimd")


class Sched:

    def __init__(self):
        self.ops = {e: [] for e in ENGINES}
        self.cnt = {}
        self.handles = {}

    def add_sem(self, name, handle):
        self.handles[name] = handle
        self.cnt[name] = 0

    CHAIN = {"vector": "s_dv", "scalar": "s_sc", "gpsimd": "s_gp", "tensor": "s_pe"}

    def op(self, eng, fn, wait=(), inc=None, k=1):
        n = None
        wait = list(wait)
        ch = self.CHAIN.get(eng)
        if ch is not None and k != 16:
            assert inc is None or inc == ch
            wait.append((ch, self.cnt[ch]))
            inc = ch
            k = 1
        if inc is not None:
            if k == 16:
                wait.append((inc, self.cnt[inc]))
            self.cnt[inc] += k
            n = self.cnt[inc]
        self.ops[eng].append((fn, tuple(wait), inc, k))
        return n

    def run(self, eng_name, eng):
        maxw = {}
        for fn, waits, inc, k in self.ops[eng_name]:
            for s, v in waits:
                if v > 0 and v > maxw.get(s, 0):
                    eng.wait_ge(self.handles[s], v)
                    maxw[s] = v
            instr = fn(eng)
            if inc is not None:
                if instr is None:
                    instr = eng.nop()
                instr.then_inc(self.handles[inc], k)


def build_program(phases=("ffn1", "attn", "ffn2")):
    do_ffn1 = "ffn1" in phases
    do_attn = "attn" in phases
    do_ffn2 = "ffn2" in phases

    nc = bass.Bass("TRN2", target_bir_lowering=False)

    x_in = nc.dram_tensor("x", [NSEQ * SEQ, D], F32, kind="ExternalInput").ap()
    out = nc.dram_tensor("out", [NSEQ * SEQ, D], F32, kind="ExternalOutput").ap()
    wts = nc.dram_tensor("wts", [D, D], F32, kind="ExternalInput").ap()
    wte = nc.dram_tensor("wte", [D, D], F32, kind="ExternalInput").ap()
    wtq = nc.dram_tensor("wtq", [D, D], F32, kind="ExternalInput").ap()
    wtk = nc.dram_tensor("wtk", [D, D], F32, kind="ExternalInput").ap()
    wtv = nc.dram_tensor("wtv", [D, D], F32, kind="ExternalInput").ap()
    wtp = nc.dram_tensor("wtp", [D, D], F32, kind="ExternalInput").ap()
    wg = nc.dram_tensor("wg", [D, 7], F32, kind="ExternalInput").ap()
    b_q = nc.dram_tensor("b_q", [1, D], F32, kind="ExternalInput").ap()
    b_k = nc.dram_tensor("b_k", [1, D], F32, kind="ExternalInput").ap()
    b_v = nc.dram_tensor("b_v", [1, D], F32, kind="ExternalInput").ap()
    b_p = nc.dram_tensor("b_p", [1, D], F32, kind="ExternalInput").ap()
    b_f = nc.dram_tensor("b_f", [1, D], F32, kind="ExternalInput").ap()
    wscr = nc.dram_tensor("wscr", [3, 2, 128, DCN, 512], BF16, kind="Internal").ap()

    S = Sched()

    with contextlib.ExitStack() as st:
        def sb(name, shape, dt):
            return st.enter_context(nc.sbuf_tensor(name, shape, dt))

        def mksem(name):
            S.add_sem(name, st.enter_context(nc.semaphore(name)))

        WS_T = sb("WS_T", [128, DCN, D], BF16)
        WE_T = sb("WE_T", [128, DCN, D], BF16)
        WP_T = sb("WP_T", [128, DCN, D], BF16)
        WSTR = sb("WSTR", [128, 2, DCN, 512], BF16)
        WG = sb("WG", [128, DCN, 7], BF16)
        BQR = sb("BQR", [1, D], BF16)
        BKR = sb("BKR", [1, D], BF16)
        BVR = sb("BVR", [1, D], BF16)
        BF2 = sb("BF2", [1, D], BF16)
        BPR = sb("BPR", [1, D], BF16)
        TRI = sb("TRI", [128, 128], BF16)
        IDE = sb("IDE", [128, 128], BF16)
        ONESR = sb("ONESR", [1, 512], BF16)
        ONESB = sb("ONESB", [65, 64], BF16)
        ONES16 = sb("ONES16", [16, 128], BF16)
        I16F = sb("I16F", [16, 16], F32)
        BD = sb("BD", [128, DCN, 16], BF16)

        for name in ("s_ld", "s_sw", "s_st", "s_of", "s_pe", "s_dv", "s_sc", "s_gp"):
            mksem(name)

        with nc.sbuf_tensor("STG", [128, DCN, D], F32) as STG, \
             nc.sbuf_tensor("STGB", [128, DCN, D], BF16) as STGB, \
             nc.sbuf_tensor("BSTG", [1, 6, D], F32) as BSTG, \
             nc.sbuf_tensor("WGS", [128, DCN, 7], F32) as WGS:

            def emit_prep():
                def gop(fn):
                    prev = S.cnt["s_gp"]
                    S.op("gpsimd", fn, wait=(("s_gp", prev),), inc="s_gp")
                gop(lambda g: g.memset(TRI[:, :], 0.0))
                gop(lambda g: g.affine_select(out=TRI[:, :], in_=TRI[:, :],
                                              compare_op=ALU.is_gt, fill=1.0, base=0,
                                              pattern=[[-1, 128]], channel_multiplier=1))
                gop(lambda g: g.memset(ONESR[:, :], 1.0))
                gop(lambda g: g.memset(ONESB[64:65, :], 1.0))
                gop(lambda g: g.memset(ONES16[:, :], 1.0))
                gop(lambda g: g.memset(IDE[:, :], 0.0))
                gop(lambda g: g.affine_select(out=IDE[:, :], in_=IDE[:, :],
                                              compare_op=ALU.not_equal, fill=1.0, base=0,
                                              pattern=[[-1, 128]], channel_multiplier=1))
                gop(lambda g: g.memset(I16F[:, :], 0.0))
                gop(lambda g: g.affine_select(out=I16F[:, :], in_=I16F[:, :],
                                              compare_op=ALU.not_equal, fill=1.0, base=0,
                                              pattern=[[-1, 16]], channel_multiplier=1))
                gop(lambda g: g.memset(BD[:, :, :], 0.0))

                def bdsets(g):
                    instr = None
                    for dc in range(DCN):
                        g.memset(BD[0:64, dc, 2 * dc:2 * dc + 1], 1.0)
                        instr = g.memset(BD[64:128, dc, 2 * dc + 1:2 * dc + 2], 1.0)
                    return instr
                gop(bdsets)

                S.op("sync", lambda e: e.dma_start(out=BSTG[0:1, 0, :], in_=b_q), inc="s_ld", k=16)
                S.op("sync", lambda e: e.dma_start(out=BSTG[0:1, 1, :], in_=b_k), inc="s_ld", k=16)
                S.op("sync", lambda e: e.dma_start(out=BSTG[0:1, 2, :], in_=b_v), inc="s_ld", k=16)
                S.op("sync", lambda e: e.dma_start(out=BSTG[0:1, 3, :], in_=b_f), inc="s_ld", k=16)
                S.op("sync", lambda e: e.dma_start(out=BSTG[0:1, 4, :], in_=b_p), inc="s_ld", k=16)
                bias_dma = S.op("sync", lambda e: e.dma_start(
                    out=WGS[:, :, :], in_=wg.rearrange("(c p) g -> p c g", p=128)), inc="s_ld", k=16)
                S.op("vector", lambda e: e.tensor_scalar_mul(BQR[0:1, :], BSTG[0:1, 0, :], 2.0 ** -1.5),
                     wait=(("s_ld", bias_dma),))
                S.op("vector", lambda e: e.tensor_scalar_mul(BKR[0:1, :], BSTG[0:1, 1, :], 2.0 ** -1.5))
                S.op("vector", lambda e: e.tensor_copy(BVR[0:1, :], BSTG[0:1, 2, :]))
                S.op("vector", lambda e: e.tensor_scalar_mul(BF2[:, :], BSTG[0:1, 3, :], 0.5))
                S.op("vector", lambda e: e.tensor_copy(BPR[:, :], BSTG[0:1, 4, :]))
                S.op("vector", lambda e: e.tensor_copy(WG[:, :, :], WGS[:, :, :]), inc="s_dv")

                RS8 = 2.0 ** -1.5
                plan = [(wts, WS_T, 0.5, None), (wte, WE_T, 0.5, None),
                        (wtp, WP_T, 1.0, None), (wtq, None, RS8, 0),
                        (wtk, None, RS8, 1), (wtv, None, 1.0, 2)]
                prev_cast = 0
                prev_store = 0
                for i, (src, dst, scale, scr_i) in enumerate(plan):
                    ld = S.op("sync", lambda e, s=src: e.dma_start(
                        out=STG[:, :, :], in_=s.rearrange("(c p) o -> p c o", p=128)),
                        wait=(("s_dv", prev_cast),), inc="s_ld", k=16)
                    if dst is not None:
                        prev_cast = S.op("vector", lambda e, d=dst, sc=scale: e.tensor_scalar_mul(
                            d[:, :, :], STG[:, :, :], sc), wait=(("s_ld", ld),), inc="s_dv")
                    else:
                        prev_cast = S.op("vector", lambda e, sc=scale: e.tensor_scalar_mul(
                            STGB[:, :, :], STG[:, :, :], sc),
                            wait=(("s_ld", ld), ("s_ld", prev_store)), inc="s_dv")
                        for half in range(2):
                            prev_store = S.op("sync", lambda e, j=scr_i, h=half: e.dma_start(
                                out=wscr[j, h], in_=STGB[:, :, h * 512:(h + 1) * 512]),
                                wait=(("s_dv", prev_cast),), inc="s_ld", k=16)
                return S.cnt["s_dv"], S.cnt["s_ld"]

            prep_dv, prep_ld = emit_prep()

        KT = sb("KT", [128, HPN, SEQ], BF16)
        VB = sb("VB", [128, 16, NH, 65], BF16)
        QT = sb("QT", [128, HPN, CH], BF16)
        XC = sb("XC", [128, NT, D], F32)
        XN = sb("XN", [128, 3, D], BF16)
        XNT = sb("XNT", [128, DCN, CH], BF16)
        OT = sb("OT", [128, HPN, CH], BF16)
        OTMP = sb("OTMP", [64, 512], BF16)
        PT = sb("PT", [128, 8, 512], BF16)
        JUNK = sb("JUNK", [128, 512], F32)
        SQS = sb("SQS", [128, 2, 512], BF16)
        RBB = sb("RBB", [64, 512], F32)
        SGR = sb("SGR", [65, 512], F32)
        RGB = sb("RGB", [65, 2, 512], BF16)
        BNS = sb("BNS", [128, NT, 12], F32)
        STAT = sb("STAT", [128, 6, NT], F32)
        SMW = sb("SMW", [128, NT, 4], F32)
        GD = sb("GD", [128, NT, 8], F32)
        GEX = sb("GEX", [128, NT, 8], F32)
        GW = sb("GW", [128, 13], F32)
        GC = sb("GC", [128, 8], F32)
        GGATE = sb("GGATE", [128, NT], F32)
        NRM = sb("NRM", [16, 8], F32)
        D16 = sb("D16", [16, 16], BF16)
        MBC = sb("MBC", [128, 16], F32)
        MBCC = sb("MBCC", [128, 1], F32)

        PSA = st.enter_context(nc.psum_tensor("PSA", [128, 4096], F32))

        def bank(b, w=512, off=0):
            return PSA[:, b * 512 + off: b * 512 + off + w]

        S.op("gpsimd", lambda e: e.memset(VB[:, :, :, 64:65], 1.0),
             wait=(("s_dv", prep_dv), ("s_ld", prep_ld), ("s_gp", S.cnt["s_gp"])), inc="s_gp")
        gp_done = S.cnt["s_gp"]

        load_ms = {}
        store_ms = {}
        trans_ms = {}
        xnt_free = [0]
        xn_free = {}
        xn_rot = [0]
        tslot_free = {0: 0, 1: 0}

        for t in range(NT):
            load_ms[(0, t)] = S.op("sync", lambda e, t=t: e.dma_start(
                out=XC[:, t, :], in_=x_in[t * 128:(t + 1) * 128, :]),
                wait=(("s_dv", prep_dv), ("s_gp", gp_done)) if t == 0 else (),
                inc="s_ld", k=16)

        def make_ln(g, ln):
            def stats(t):
                w = []
                if ln == 0:
                    w.append(("s_ld", load_ms[(g, t)]))
                S.op("vector", lambda e, t=t: e.bn_stats(BNS[:, t, 0:6], XC[:, t, 0:512]), wait=w)
                S.op("vector", lambda e, t=t: e.bn_stats(BNS[:, t, 6:12], XC[:, t, 512:1024]))
                mvt = BNS[:, t, 1:12:3]
                vvt = BNS[:, t, 2:12:3]
                S.op("vector", lambda e, t=t, mvt=mvt: e.tensor_mul(SMW[:, t, :], mvt, mvt))
                S.op("vector", lambda e, t=t, vvt=vvt: e.scalar_tensor_tensor(
                    SMW[:, t, :], SMW[:, t, :], 256.0, vvt, ALU.mult, ALU.add))
                S.op("vector", lambda e, t=t: e.tensor_reduce(
                    STAT[:, 5, t:t + 1], SMW[:, t, :], axis=AX.X, op=ALU.add))
                S.op("vector", lambda e, t=t, mvt=mvt: e.tensor_reduce(
                    STAT[:, 0, t:t + 1], mvt, axis=AX.X, op=ALU.add))
                S.op("vector", lambda e, t=t: e.tensor_scalar_mul(
                    STAT[:, 0, t:t + 1], STAT[:, 0, t:t + 1], 0.25))
                S.op("vector", lambda e, t=t: e.tensor_mul(
                    STAT[:, 1, t:t + 1], STAT[:, 0, t:t + 1], STAT[:, 0, t:t + 1]))
                S.op("vector", lambda e, t=t: e.scalar_tensor_tensor(
                    STAT[:, 2, t:t + 1], STAT[:, 5, t:t + 1], 1.0 / 1024.0,
                    STAT[:, 1, t:t + 1], ALU.mult, ALU.subtract))
                return S.op("vector", lambda e, t=t: e.tensor_scalar(
                    STAT[:, 2, t:t + 1], STAT[:, 2, t:t + 1], EPS, None, ALU.add))

            def lnexp(t, dvv):
                S.op("scalar", lambda e, t=t: e.activation(
                    STAT[:, 3, t:t + 1], STAT[:, 2, t:t + 1], AF.Ln),
                    wait=(("s_dv", dvv),))
                return S.op("scalar", lambda e, t=t: e.activation(
                    STAT[:, 4, t:t + 1], STAT[:, 3, t:t + 1], AF.Exp, scale=-0.5))

            slot_of_t = {}

            def apply_t(t, scm):
                slot = xn_rot[0] % 3
                xn_rot[0] += 1
                slot_of_t[t] = slot
                w = [("s_sc", scm)] + list(xn_free.get(slot, ()))
                ap_ms = S.op("vector", lambda e, t=t, slot=slot: e.tensor_scalar(
                    XN[:, slot, :], XC[:, t, :], STAT[:, 0, t:t + 1], STAT[:, 4, t:t + 1],
                    ALU.subtract, ALU.mult), wait=w)
                return ap_ms

            def transpose_t(t, ap_ms):
                slot = slot_of_t[t]
                tb = (6 + t % 2) * 512
                last_pe = 0
                for dc in range(DCN):
                    pv = PSA[:, tb + dc * 64:tb + dc * 64 + 64].bitcast(BF16)
                    last_pe = S.op("tensor", lambda e, slot=slot, dc=dc, pv=pv: e.transpose(
                        pv, XN[:, slot, dc * 128:(dc + 1) * 128], IDE[:, :]),
                        wait=(("s_dv", ap_ms), ("s_dv", tslot_free[t % 2])), inc="s_pe")
                cp = S.op("vector", lambda e, t=t, tb=tb: e.tensor_copy(
                    XNT[:, :, t * 128:(t + 1) * 128],
                    PSA[:, tb:tb + 512].bitcast(BF16).rearrange(
                        "p (c f) -> p c f", c=DCN)),
                    wait=(("s_pe", last_pe), ("s_pe", xnt_free[0])))
                tslot_free[t % 2] = cp
                trans_ms[(g, ln, t)] = (("s_dv", cp),)
                xn_free[slot] = (("s_pe", last_pe),)
                return cp

            exps = {}
            applied = {}

            def pre(t):
                if t not in exps:
                    exps[t] = lnexp(t, stats(t))

            def full(t):
                if t in applied:
                    return
                pre(t)
                applied[t] = True
                transpose_t(t, apply_t(t, exps[t]))

            return {"pre": pre, "full": full, "exps": exps, "applied": applied,
                    "apply": apply_t, "transpose": transpose_t}

        def emit_ffn(g, ln, entry_dv, lnh, tail=None):
            evac_ms = {}
            gate_ms = {}
            mm_ms = {}
            for t in range(NT):
                lnh["pre"](t)
            ap = {}
            for t in range(3):
                if t not in lnh["applied"]:
                    lnh["applied"][t] = True
                    ap[t] = lnh["apply"](t, lnh["exps"][t])
            if 0 in ap:
                lnh["transpose"](0, ap[0])
            for u, (t, oh) in enumerate([(t, oh) for t in range(NT) for oh in range(2)]):
                slot = u % 2
                bs, be = slot * 2, slot * 2 + 1
                gb = 4 + (t % 2)
                common = list(trans_ms[(g, ln, t)])
                if u >= 2:
                    common.append(("s_dv", evac_ms[u - 2]))
                else:
                    common.append(("s_dv", entry_dv))
                if oh == 0 and t >= 2:
                    common.append(gate_ms[t - 2])

                def mms(e, t=t, oh=oh, bs=bs, be=be, gb=gb):
                    tl = slice(t * 128, (t + 1) * 128)
                    ohs = slice(oh * 512, (oh + 1) * 512)
                    for dc in range(DCN):
                        e.matmul(bank(bs), XNT[:, dc, tl], WS_T[:, dc, ohs],
                                 start=(dc == 0), stop=False)
                    e.matmul(bank(bs), ONESR[0:1, 0:128], BF2[0:1, ohs],
                             start=False, stop=True)
                    for dc in range(DCN):
                        instr = e.matmul(bank(be), XNT[:, dc, tl], WE_T[:, dc, ohs],
                                         start=(dc == 0), stop=(dc == DCN - 1))
                    if oh == 0:
                        for dc in range(DCN):
                            instr = e.matmul(bank(gb, 7, (t // 2) * 8), XNT[:, dc, tl],
                                             WG[:, dc, :], start=(dc == 0),
                                             stop=(dc == DCN - 1))
                    return instr
                mm_ms[u] = S.op("tensor", mms, wait=common, inc="s_pe")

                if oh == 0 and t + 1 < NT:
                    if (t + 1) in ap:
                        lnh["transpose"](t + 1, ap[t + 1])
                    else:
                        lnh["full"](t + 1)
                    if t == 0 and 3 not in lnh["applied"]:
                        lnh["applied"][3] = True
                        ap[3] = lnh["apply"](3, lnh["exps"][3])

                if oh == 0:
                    gv = bank(gb, 7, (t // 2) * 8)
                    S.op("vector", lambda e, gv=gv: e.tensor_reduce(
                        GC[:, 0:1], gv, axis=AX.X, op=ALU.max), wait=(("s_pe", mm_ms[u]),))
                    d1 = S.op("vector", lambda e, gv=gv, t=t: e.tensor_scalar(
                        GD[:, t, 0:7], gv, GC[:, 0:1], None, ALU.subtract))
                    s1 = S.op("scalar", lambda e, t=t: e.activation(
                        GEX[:, t, 0:7], GD[:, t, 0:7], AF.Exp), wait=(("s_dv", d1),), inc="s_sc")
                    S.op("vector", lambda e, t=t: e.tensor_reduce(
                        GC[:, 1:2], GEX[:, t, 0:7], axis=AX.X, op=ALU.add), wait=(("s_sc", s1),))
                    S.op("vector", lambda e, t=t: e.tensor_single_scalar(
                        GW[:, 0:7], GD[:, t, 0:7], 0.0, ALU.is_equal))
                    S.op("vector", lambda e, t=t: e.tensor_sub(GW[:, 0:7], GEX[:, t, 0:7], GW[:, 0:7]))
                    S.op("vector", lambda e: e.tensor_reduce(
                        GC[:, 2:3], GW[:, 0:7], axis=AX.X, op=ALU.max))
                    S.op("vector", lambda e: e.tensor_scalar(
                        GC[:, 3:4], GC[:, 2:3], 1.0, None, ALU.add))
                    S.op("vector", lambda e: e.reciprocal(GC[:, 5:6], GC[:, 1:2]))
                    S.op("vector", lambda e: e.tensor_mul(GC[:, 4:5], GC[:, 3:4], GC[:, 5:6]))
                    S.op("vector", lambda e: e.reciprocal(GC[:, 6:7], GC[:, 4:5]))
                    gate_ms[t] = ("s_dv", S.op("vector", lambda e, t=t: e.tensor_mul(
                        GGATE[:, t:t + 1], GC[:, 4:5], GC[:, 6:7])))

                ohs = slice(oh * 512, (oh + 1) * 512)
                S.op("vector", lambda e, t=t, be=be, ohs=ohs: e.scalar_tensor_tensor(
                    JUNK[:, :], bank(be), GGATE[:, t:t + 1], XC[:, t, ohs],
                    ALU.mult, ALU.add), wait=(("s_pe", mm_ms[u]), gate_ms[t]))
                evac_ms[u] = S.op("vector", lambda e, t=t, bs=bs, ohs=ohs: e.scalar_tensor_tensor(
                    XC[:, t, ohs], bank(bs), 1.0, JUNK[:, :], ALU.mult, ALU.add))
                if oh == 1 and tail is not None:
                    tail(t, evac_ms[u])
            xnt_free[0] = S.cnt["s_pe"]
            return {t: evac_ms[2 * t + 1] for t in range(NT)}

        def emit_qkv(g, slot_free, entry_dv):
            c = g % NCH
            ld = {}

            def stream(i, widx, half, wait_pe):
                ld[i] = S.op("scalar", lambda e, widx=widx, half=half, i=i: e.dma_start(
                    out=WSTR[:, i % 2, :, :], in_=wscr[widx, half]),
                    wait=(("s_pe", wait_pe), ("s_ld", prep_ld)), inc="s_sw", k=16)

            stream(0, 0, 0, slot_free[0])
            stream(1, 0, 1, slot_free[1])

            qmm, qev = {}, {}
            for hp in range(HPN):
                b = hp % 2
                w = [("s_sw", ld[0 if hp < 4 else 1])] + list(trans_ms[(g, 1, "all")])
                if hp >= 2:
                    w.append(qev[hp - 2])
                else:
                    w.append(("s_dv", entry_dv))

                def qm(e, hp=hp, b=b):
                    sl = 0 if hp < 4 else 1
                    col = slice((hp % 4) * 128, (hp % 4 + 1) * 128)
                    for dc in range(DCN):
                        e.matmul(bank(b), WSTR[:, sl, dc, col], XNT[:, dc, :],
                                 start=(dc == 0), stop=False)
                    return e.matmul(bank(b), BQR[0:1, hp * 128:(hp + 1) * 128],
                                    ONESR[0:1, 0:512], start=False, stop=True)
                qmm[hp] = S.op("tensor", qm, wait=w, inc="s_pe")
                qev[hp] = ("s_dv", S.op("vector", lambda e, hp=hp, b=b: e.tensor_copy(
                    QT[:, hp, :], bank(b)), wait=(("s_pe", qmm[hp]),)))

            stream(2, 1, 0, qmm[3])
            stream(3, 1, 1, qmm[7])

            kmm, kev = {}, {}
            for hp in range(HPN):
                b = 2 + hp % 2
                w = [("s_sw", ld[2 if hp < 4 else 3])]
                if hp >= 2:
                    w.append(kev[hp - 2])

                def km(e, hp=hp, b=b):
                    sl = 0 if hp < 4 else 1
                    col = slice((hp % 4) * 128, (hp % 4 + 1) * 128)
                    for dc in range(DCN):
                        e.matmul(bank(b), WSTR[:, sl, dc, col], XNT[:, dc, :],
                                 start=(dc == 0), stop=False)
                    return e.matmul(bank(b), BKR[0:1, hp * 128:(hp + 1) * 128],
                                    ONESR[0:1, 0:512], start=False, stop=True)
                kmm[hp] = S.op("tensor", km, wait=w, inc="s_pe")
                kev[hp] = ("s_dv", S.op("vector", lambda e, hp=hp, b=b, c=c: e.tensor_copy(
                    KT[:, hp, c * 512:(c + 1) * 512], bank(b)),
                    wait=(("s_pe", kmm[hp]),)))

            stream(4, 2, 0, kmm[3])
            stream(5, 2, 1, kmm[7])

            vmm, vev = {}, {}
            for t in range(NT):
                pb = 4 + 2 * (t % 2)
                w = [("s_sw", ld[5])]
                if t >= 2:
                    w.append(vev[t - 2])

                def vm(e, t=t, pb=pb):
                    tl = slice(t * 128, (t + 1) * 128)
                    instr = None
                    for half in range(2):
                        for dc in range(DCN):
                            e.matmul(bank(pb + half), XNT[:, dc, tl], WSTR[:, half, dc, :],
                                     start=(dc == 0), stop=False)
                        instr = e.matmul(bank(pb + half), ONESR[0:1, 0:128],
                                         BVR[0:1, half * 512:(half + 1) * 512],
                                         start=False, stop=True)
                    return instr
                vmm[t] = S.op("tensor", vm, wait=w, inc="s_pe")
                jt = c * 4 + t
                vev[t] = ("s_dv", S.op("vector", lambda e, jt=jt, pb=pb: e.tensor_copy(
                    VB[:, jt, :, 0:64],
                    PSA[:, pb * 512:(pb + 2) * 512].rearrange("p (h v) -> p h v", h=16)),
                    wait=(("s_pe", vmm[t]),)))

            slot_free[0] = vmm[NT - 1]
            slot_free[1] = vmm[NT - 1]
            xnt_free[0] = S.cnt["s_pe"]
            return qev, kev, vev

        def emit_norms(g, qev, kev, vev):
            c = g % NCH
            nmm = {}
            sq_ms = {}
            for i, which in enumerate(("q", "k")):
                for hp in range(HPN):
                    gidx = i * 8 + hp
                    w = [qev[hp] if which == "q" else kev[hp]]
                    if gidx >= 2:
                        pi, php = divmod(gidx - 2, 8)
                        w.append(("s_pe", nmm[(pi, php)]))
                    src = (lambda hp=hp: QT[:, hp, :]) if which == "q" else \
                          (lambda hp=hp, c=c: KT[:, hp, c * 512:(c + 1) * 512])
                    sq_ms[(i, hp)] = S.op("vector", lambda e, hp=hp, src=src: e.tensor_mul(
                        SQS[:, hp % 2, :], src(), src()), wait=w)
                    wpe = [("s_dv", sq_ms[(i, hp)])]
                    if i == 0 and hp < 2:
                        wpe.append(vev[NT - 1])
                    nmm[(i, hp)] = S.op("tensor", lambda e, i=i, hp=hp: e.matmul(
                        PSA[0:16, i * 512:(i + 1) * 512], BD[:, hp, :], SQS[:, hp % 2, :],
                        start=(hp == 0), stop=(hp == HPN - 1)), wait=wpe, inc="s_pe")
            S.op("vector", lambda e: e.reduce_max(NRM[:, 0:1], PSA[0:16, 0:512], axis=AX.X),
                 wait=(("s_pe", nmm[(1, HPN - 1)]),))
            S.op("vector", lambda e: e.reduce_max(NRM[:, 1:2], PSA[0:16, 512:1024], axis=AX.X))
            if c == 0:
                S.op("vector", lambda e: e.tensor_copy(NRM[:, 2:3], NRM[:, 1:2]))
            else:
                S.op("vector", lambda e, c=c: e.tensor_tensor(
                    NRM[:, 2 + c % 2:3 + c % 2], NRM[:, 2 + (c + 1) % 2:3 + (c + 1) % 2],
                    NRM[:, 1:2], ALU.max))
            S.op("vector", lambda e, c=c: e.tensor_add(NRM[:, 4:5], NRM[:, 0:1],
                                                       NRM[:, 2 + c % 2:3 + c % 2]))
            dvm = S.op("vector", lambda e: e.tensor_scalar(
                D16[:, :], I16F[:, :], NRM[:, 4:5], -0.55, ALU.mult, ALU.mult), inc="s_dv")
            mh = S.op("tensor", lambda e: e.matmul(
                PSA[:, 1024:1040], ONES16[:, :], D16[:, :], start=True, stop=True),
                wait=(("s_dv", dvm),), inc="s_pe")
            cp = S.op("scalar", lambda e: e.copy(MBC[:, :], PSA[:, 1024:1040]),
                      wait=(("s_pe", mh),), inc="s_sc")
            mbc = S.op("vector", lambda e: e.tensor_reduce(
                MBCC[:, :], MBC[:, :], axis=AX.X, op=ALU.min),
                wait=(("s_sc", cp),))
            return mbc

        def emit_attention(g, mbc_ms, attn_state):
            c = g % NCH
            kend = 4 * (c + 1)
            JP = kend // 2
            sb_free = attn_state["sb_free"]
            pt_last_pv = attn_state["pt_last_pv"]
            o_free_dv = attn_state["o_free_dv"]
            r_free_sc = attn_state["r_free_sc"]
            of_last = attn_state["of_last"]
            pend = attn_state["pend"]

            def off_of(j):
                return (j - 4 * c) * 128 if j >= 4 * c else 0

            def ob(h):
                return 4 + h % 2

            def rb(h):
                return 6 + h % 2

            def emit_ss(he, ho, jp):
                hp = he // 2
                w = [("s_sc", sb_free[0]), ("s_dv", mbc_ms)]

                def fn(e, hp=hp, jp=jp):
                    instr = None
                    for h in (he, ho):
                        hb = 64 * (h % 2)
                        for j in (2 * jp, 2 * jp + 1):
                            off = off_of(j)
                            b = 2 * (h % 2) + (j % 2)
                            instr = e.matmul(bank(b, 512 - off, off),
                                             KT[hb:hb + 64, hp, j * 128:(j + 1) * 128],
                                             QT[hb:hb + 64, hp, off:512],
                                             start=True, stop=True)
                    return instr
                return S.op("tensor", fn, wait=w, inc="s_pe")

            def emit_expb(jp, s_ms):
                w = [("s_pe", s_ms), ("s_pe", pt_last_pv.get(jp % 2, 0))]
                o = PT.rearrange("p (a b) f -> p a b f", a=2)[:, :, 2 * (jp % 2):2 * (jp % 2) + 2, :]
                i = PSA[:, 0:2048].rearrange("p (a b c) -> p a b c", a=2, b=2)
                return S.op("scalar", lambda e, o=o, i=i: e.activation(
                    o, i, AF.Exp, bias=MBCC[:, :]), wait=w, inc="s_sc")

            def emit_masks(h, jp, e_ms):
                ms = 0
                for j in (2 * jp, 2 * jp + 1):
                    if j >= 4 * c:
                        off = off_of(j)
                        sl = 4 * (h % 2) + (j % 4)
                        ms = S.op("vector", lambda e, sl=sl, off=off: e.tensor_mul(
                            PT[:, sl, off:off + 128], PT[:, sl, off:off + 128], TRI[:, :]),
                            wait=(("s_sc", e_ms),))
                return ms

            def emit_pvv(he, ho, jp, ee, eo, me, mo):
                w = [("s_sc", ee), ("s_sc", eo)]
                if me:
                    w.append(("s_dv", me))
                if mo:
                    w.append(("s_dv", mo))
                if jp == 0:
                    w.append(("s_dv", o_free_dv[0]))
                    w.append(("s_dv", o_free_dv[1]))

                def fn(e, he=he, ho=ho, jp=jp):
                    instr = None
                    for h in (he, ho):
                        for j in (2 * jp, 2 * jp + 1):
                            off = off_of(j)
                            sl = 4 * (h % 2) + (j % 4)
                            instr = e.matmul(
                                PSA[0:65, ob(h) * 512 + off:(ob(h) + 1) * 512],
                                VB[:, j, h, :], PT[:, sl, off:512],
                                start=(j == 0), stop=(j == kend - 1))
                    return instr
                pv = S.op("tensor", fn, wait=w, inc="s_pe")
                pt_last_pv[jp % 2] = pv
                return pv

            def emit_sg_chain(h, pv_ms):
                S.op("scalar", lambda e, h=h: e.activation(
                    SGR[64:65, 0:512], PSA[64:65, ob(h) * 512:(ob(h) + 1) * 512], AF.Ln),
                    wait=(("s_pe", pv_ms),))
                rgb = S.op("scalar", lambda e, h=h: e.activation(
                    RGB[64:65, h % 2, :], SGR[64:65, 0:512], AF.Exp, scale=-1.0))
                return {"h": h, "rgb_sc": rgb}

            def emit_finalize(p, rbb_free_dv):
                h = p["h"]
                r1 = S.op("tensor", lambda e, h=h: e.matmul(
                    PSA[0:64, rb(h) * 512:(rb(h) + 1) * 512], ONESB[64:65, :],
                    RGB[64:65, h % 2, :], start=True, stop=True),
                    wait=(("s_sc", p["rgb_sc"]), r_free_sc[h % 2]), inc="s_pe")
                rbc = S.op("vector", lambda e, h=h: e.tensor_copy(
                    RBB[:, :], PSA[0:64, rb(h) * 512:(rb(h) + 1) * 512]),
                    wait=(("s_pe", r1), ("s_dv", rbb_free_dv)))
                r_free_sc[h % 2] = ("s_dv", rbc)
                hp, hb = h // 2, 64 * (h % 2)
                if hb == 0:
                    ot = S.op("vector", lambda e, hp=hp, h=h: e.tensor_tensor(
                        OT[0:64, hp, :], PSA[0:64, ob(h) * 512:(ob(h) + 1) * 512],
                        RBB[:, :], ALU.mult))
                else:
                    ot = S.op("vector", lambda e, h=h: e.tensor_tensor(
                        OTMP[:, :], PSA[0:64, ob(h) * 512:(ob(h) + 1) * 512],
                        RBB[:, :], ALU.mult),
                        wait=(("s_of", of_last[0]),))
                    of_last[0] = S.op("sync", lambda e, hp=hp: e.dma_start(
                        out=OT[64:128, hp, :], in_=OTMP[:, :]),
                        wait=(("s_dv", ot),), inc="s_of", k=16)
                o_free_dv[h % 2] = ot
                return ot

            ot_last = 0
            for hp2 in range(HPN):
                he, ho = 2 * hp2, 2 * hp2 + 1
                s_ms = emit_ss(he, ho, 0)
                if pend[0] is not None:
                    pe_, po_ = pend[0]
                    o1 = emit_finalize(pe_, 0)
                    ot_last = emit_finalize(po_, o1)
                    pend[0] = None
                ee = emit_expb(0, s_ms)
                me = emit_masks(he, 0, ee)
                mo = emit_masks(ho, 0, ee)
                sb_free[0] = ee
                prev = (0, ee, ee, me, mo)
                for jp in range(1, JP):
                    s_ms = emit_ss(he, ho, jp)
                    ee = emit_expb(jp, s_ms)
                    me = emit_masks(he, jp, ee)
                    mo = emit_masks(ho, jp, ee)
                    sb_free[0] = ee
                    pjp, pee, peo, pme, pmo = prev
                    emit_pvv(he, ho, pjp, pee, peo, pme, pmo)
                    prev = (jp, ee, ee, me, mo)
                pjp, pee, peo, pme, pmo = prev
                pv = emit_pvv(he, ho, pjp, pee, peo, pme, pmo)
                pend[0] = (emit_sg_chain(he, pv), emit_sg_chain(ho, pv))
            pe_, po_ = pend[0]
            o1 = emit_finalize(pe_, 0)
            ot_last = emit_finalize(po_, o1)
            pend[0] = None
            return ot_last

        def emit_delta(g, ot_dv, tail=None):
            ev = {}
            sc_now = S.cnt["s_sc"]
            of_now = S.cnt["s_of"]
            for t in range(NT):
                pb = 2 * (t % 2)
                w = [("s_dv", ot_dv), ("s_of", of_now)]
                if t >= 2:
                    w.append(("s_dv", ev[t - 2]))
                else:
                    w.append(("s_sc", sc_now))

                def dm(e, t=t, pb=pb):
                    tl = slice(t * 128, (t + 1) * 128)
                    instr = None
                    for half in range(2):
                        for hp in range(DCN):
                            e.matmul(bank(pb + half), OT[:, hp, tl],
                                     WP_T[:, hp, half * 512:(half + 1) * 512],
                                     start=(hp == 0), stop=False)
                        instr = e.matmul(bank(pb + half), ONESR[0:1, 0:128],
                                         BPR[0:1, half * 512:(half + 1) * 512],
                                         start=False, stop=True)
                    return instr
                m = S.op("tensor", dm, wait=w, inc="s_pe")

                S.op("vector", lambda e, t=t, pb=pb: e.tensor_add(
                    XC[:, t, 0:512], XC[:, t, 0:512], bank(pb)), wait=(("s_pe", m),))
                ev[t] = S.op("vector", lambda e, t=t, pb=pb: e.tensor_add(
                    XC[:, t, 512:1024], XC[:, t, 512:1024], bank(pb + 1)))
                if tail is not None:
                    tail(t, ev[t])
            return ev

        def store_load_t(g, t, done_ms):
            base = g * CH + t * 128
            store_ms[(g, t)] = S.op("sync", lambda e, base=base, t=t: e.dma_start(
                out=out[base:base + 128, :], in_=XC[:, t, :]),
                wait=(("s_dv", done_ms),), inc="s_st", k=16)
            if g + 1 < NCHG:
                nbase = (g + 1) * CH + t * 128
                load_ms[(g + 1, t)] = S.op("sync", lambda e, nbase=nbase, t=t: e.dma_start(
                    out=XC[:, t, :], in_=x_in[nbase:nbase + 128, :]),
                    wait=(("s_st", store_ms[(g, t)]),), inc="s_ld", k=16)

        slot_free = [0, 0]
        attn_state = {"sb_free": {0: 0, 1: 0},
                      "pt_last_pv": {},
                      "o_free_dv": {0: 0, 1: 0},
                      "r_free_sc": {0: ("s_dv", 0), 1: ("s_dv", 0)},
                      "of_last": [0], "pend": [None]}
        LNH = {}

        def get_lnh(g, ln):
            if (g, ln) not in LNH:
                LNH[(g, ln)] = make_ln(g, ln)
            return LNH[(g, ln)]

        ln_next = {0: (1 if do_attn else (2 if do_ffn2 else None)),
                   1: (2 if do_ffn2 else None), 2: None}

        def mk_tail(g, ln):
            nxt = ln_next[ln]
            if nxt is not None:
                if nxt == 1:
                    def attn_tail(t, dv_ms):
                        h = get_lnh(g, 1)
                        h["pre"](t)
                        if t < 3 and t not in h["applied"]:
                            h["applied"][t] = True
                            h.setdefault("ap", {})[t] = h["apply"](t, h["exps"][t])
                    return attn_tail
                return lambda t, dv_ms: get_lnh(g, nxt)["pre"](t)

            def last_tail(t, dv_ms):
                store_load_t(g, t, dv_ms)
                if g + 1 < NCHG:
                    get_lnh(g + 1, 0)["pre"](t)
            return last_tail

        for g in range(NCHG):
            done = None
            guard = S.cnt["s_dv"]
            if do_ffn1:
                done = emit_ffn(g, 0, guard, get_lnh(g, 0), mk_tail(g, 0))
            elif do_attn or do_ffn2:
                pass
            if do_attn:
                guard = S.cnt["s_dv"]
                lnh2 = get_lnh(g, 1)
                ap2 = lnh2.pop("ap", {})
                for t in range(NT):
                    if t in ap2:
                        lnh2["transpose"](t, ap2[t])
                        if t == 0 and 3 not in lnh2["applied"]:
                            lnh2["applied"][3] = True
                            ap2[3] = lnh2["apply"](3, lnh2["exps"][3])
                    else:
                        lnh2["full"](t)
                trans_ms[(g, 1, "all")] = tuple(
                    trans_ms[(g, 1, t)][0] for t in range(NT))
                qev, kev, vev = emit_qkv(g, slot_free, guard)
                mbc = emit_norms(g, qev, kev, vev)
                ot_dv = emit_attention(g, mbc, attn_state)
                done = emit_delta(g, ot_dv, mk_tail(g, 1))
            if do_ffn2:
                guard = S.cnt["s_dv"]
                done = emit_ffn(g, 2, guard, get_lnh(g, 2), mk_tail(g, 2))
            if done is None:
                done = {t: S.cnt["s_dv"] for t in range(NT)}
            for t in range(NT):
                if (g, t) not in store_ms:
                    store_load_t(g, t, done[t] if isinstance(done, dict) else S.cnt["s_dv"])
                    if g + 1 < NCHG:
                        get_lnh(g + 1, 0)["pre"](t)

        fw = [("s_st", S.cnt["s_st"]), ("s_ld", S.cnt["s_ld"]),
              ("s_sw", S.cnt["s_sw"]), ("s_of", S.cnt["s_of"])]
        S.op("sync", lambda e: e.nop(), wait=tuple(fw))

        with nc.Block() as block:
            block.tensor(lambda e: S.run("tensor", e))
            block.vector(lambda e: S.run("vector", e))
            block.scalar(lambda e: S.run("scalar", e))
            block.sync(lambda e: S.run("sync", e))
            block.gpsimd(lambda e: S.run("gpsimd", e))

    return nc


_NC_CACHE = {}


def _get_nc(phases):
    key = tuple(sorted(phases))
    if key not in _NC_CACHE:
        _NC_CACHE[key] = build_program(phases)
    return _NC_CACHE[key]


def run_sharded(x_full, weights, phases=("ffn1", "attn", "ffn2")):
    n = 8
    nc = _get_nc(phases)
    wT = {k: np.ascontiguousarray(np.asarray(v, dtype=np.float32).T) for k, v in (
        ("wts", weights["L0_ffn_W_shared"]), ("wte", weights["L0_ffn_W_expert"]),
        ("wtq", weights["L0_attn_Wq"]), ("wtk", weights["L0_attn_Wk"]),
        ("wtv", weights["L0_attn_Wv"]), ("wtp", weights["L0_attn_Wp"]))}
    common = dict(wT)
    common["wg"] = np.ascontiguousarray(np.asarray(weights["L0_ffn_W_gate"], dtype=np.float32))
    for nm, key in (("b_q", "L0_attn_bq"), ("b_k", "L0_attn_bk"), ("b_v", "L0_attn_bv"),
                    ("b_p", "L0_attn_bp"), ("b_f", "L0_ffn_bias")):
        common[nm] = np.ascontiguousarray(np.asarray(weights[key], dtype=np.float32).reshape(1, D))
    x_full = np.asarray(x_full, dtype=np.float32)
    in_maps = []
    for i in range(n):
        m = dict(common)
        m["x"] = np.ascontiguousarray(x_full[2 * i:2 * i + 2].reshape(NSEQ * SEQ, D))
        in_maps.append(m)
    res = run_bass_kernel_spmd(nc, in_maps, core_ids=list(range(n)))
    outs = [np.asarray(r["out"]).reshape(NSEQ, SEQ, D) for r in res.results]
    return np.concatenate(outs, axis=0)


def kernel(**inputs):
    return run_sharded(inputs["L0_attn_x"], inputs)
